# version 34
# baseline (speedup 1.0000x reference)
import numpy as np
from contextlib import ExitStack

import concourse.bass as bass
import concourse.mybir as mybir
from concourse.bass_utils import run_bass_kernel_spmd

F32 = mybir.dt.float32
BF16 = mybir.dt.bfloat16
I32 = mybir.dt.int32
U32 = mybir.dt.uint32
AF = mybir.ActivationFunctionType
ALU = mybir.AluOpType
AX = mybir.AxisListType

NCORES = 8
B, N, D = 16, 4096, 1024
HID = 512
HEADS, DH = 8, 64
SCALE = DH ** -0.5
H = 512
E = 8
M = 2048
BPC = 2
TPB = N // 128
MS = M // NCORES
MT = MS // 128
EPS = 1e-5

ENGINES = ("sync", "scalar", "vector", "tensor", "gpsimd")

RACE_DEBUG = False


class Sched:
    def __init__(self):
        self.ops = {e: [] for e in ENGINES}
        self.cnt = {}

    def val(self, sem):
        return self.cnt.get(sem, 0)

    def emit(self, eng, fn, waits=(), inc=None, amt=1, drain=True):
        ws = [(s, v) for (s, v) in waits if v > 0]
        self.ops[eng].append((fn, ws, inc, amt, drain))
        if inc is not None:
            self.cnt[inc] = self.cnt.get(inc, 0) + amt
        return self.cnt.get(inc, 0) if inc is not None else None


def build_program():
    nc = bass.Bass("TRN2", target_bir_lowering=False,
                   detect_race_conditions=RACE_DEBUG)

    dx = nc.dram_tensor("x", [BPC * N, D], F32, kind="ExternalInput").ap()
    dwq = nc.dram_tensor("wq", [D, HID], F32, kind="ExternalInput").ap()
    dwk = nc.dram_tensor("wk", [D, HID], F32, kind="ExternalInput").ap()
    dwv = nc.dram_tensor("wv", [D, HID], F32, kind="ExternalInput").ap()
    dwo = nc.dram_tensor("wo", [HID, D], F32, kind="ExternalInput").ap()
    dsw1t = nc.dram_tensor("sw1t", [D, H], F32, kind="ExternalInput").ap()
    dsw2t = nc.dram_tensor("sw2t", [D, H], F32, kind="ExternalInput").ap()
    dsw3t = nc.dram_tensor("sw3t", [H, D], F32, kind="ExternalInput").ap()
    dew1t = nc.dram_tensor("ew1t", [E, D, H], F32, kind="ExternalInput").ap()
    dew2t = nc.dram_tensor("ew2t", [E, D, H], F32, kind="ExternalInput").ap()
    dew3t = nc.dram_tensor("ew3t", [E, H, D], F32, kind="ExternalInput").ap()
    deb1 = nc.dram_tensor("eb1", [E, H], F32, kind="ExternalInput").ap()
    deb2 = nc.dram_tensor("eb2", [E, H], F32, kind="ExternalInput").ap()
    deb3 = nc.dram_tensor("eb3", [E, D], F32, kind="ExternalInput").ap()
    dgsw = nc.dram_tensor("gsw", [D, E + 1], F32, kind="ExternalInput").ap()
    dgb9 = nc.dram_tensor("gb9", [128, E + 1], F32, kind="ExternalInput").ap()
    dexpb = nc.dram_tensor("expb", [128, E], F32, kind="ExternalInput").ap()
    diota8 = nc.dram_tensor("iota8", [128, E], F32, kind="ExternalInput").ap()
    diota16 = nc.dram_tensor("iota16", [128, 16], F32, kind="ExternalInput").ap()
    dselg = nc.dram_tensor("selg", [128, H], F32, kind="ExternalInput").ap()
    dselb = nc.dram_tensor("selb", [128, H], F32, kind="ExternalInput").ap()
    dslng = nc.dram_tensor("slng", [128, 4], F32, kind="ExternalInput").ap()
    dslnb = nc.dram_tensor("slnb", [128, 4], F32, kind="ExternalInput").ap()
    dsb1 = nc.dram_tensor("sb1", [1, H], F32, kind="ExternalInput").ap()
    dsb2 = nc.dram_tensor("sb2", [1, H], F32, kind="ExternalInput").ap()
    dsb3 = nc.dram_tensor("sb3", [1, D], F32, kind="ExternalInput").ap()
    dbo = nc.dram_tensor("bo", [1, D], F32, kind="ExternalInput").ap()
    didb = nc.dram_tensor("idb", [128, MT], I32, kind="ExternalInput").ap()
    dxmoe = nc.dram_tensor("xmoe", [MS, D], F32, kind="ExternalInput").ap()
    didn = nc.dram_tensor("idn", [128, 128], BF16, kind="ExternalInput").ap()
    didnf = nc.dram_tensor("idnf", [128, 128], F32, kind="ExternalInput").ap()
    donesr = nc.dram_tensor("onesr", [1, 128], BF16, kind="ExternalInput").ap()
    donesc = nc.dram_tensor("onesc", [128, 1], BF16, kind="ExternalInput").ap()

    dom = nc.dram_tensor("om", [BPC * N, D], F32, kind="ExternalOutput").ap()
    domoe = nc.dram_tensor("omoe", [MS, D], F32, kind="ExternalOutput").ap()

    cc_in = nc.dram_tensor("cc_in", [BPC, 4, 128, 128], BF16).ap()
    cc_out = nc.dram_tensor("cc_out", [B, 4, 128, 128], BF16).ap()

    es = ExitStack()
    sbuf = lambda name, shape, dt: es.enter_context(nc.sbuf_tensor(name, shape, dt))
    psum = lambda name, shape, dt: es.enter_context(nc.psum_tensor(name, shape, dt))

    wq_s = sbuf("wq_s", [128, 8, HID], BF16)
    wk_s = sbuf("wk_s", [128, 8, HID], BF16)
    wv_s = sbuf("wv_s", [128, 8, HID], BF16)
    wo_s = sbuf("wo_s", [128, 4, D], BF16)
    sw1t_s = sbuf("sw1t_s", [128, 8, H], BF16)
    sw2t_s = sbuf("sw2t_s", [128, 8, H], BF16)
    sw3g_s = sbuf("sw3g_s", [128, 4, D], BF16)
    g_s = sbuf("g_s", [128, 4, D], BF16)
    gsw_s = sbuf("gsw_s", [128, 8, E + 1], BF16)
    eb3_s = sbuf("eb3_s", [E, D], BF16)
    ew1_s = sbuf("ew1_s", [128, 2, 8, H], BF16)
    ew2_s = sbuf("ew2_s", [128, 1, 8, H], BF16)
    ew3_s = sbuf("ew3_s", [128, 1, 4, D], BF16)
    ebrow_s = sbuf("ebrow_s", [1, 2, 2, H], BF16)
    sw3stage = sbuf("sw3stage", [128, 2, D], F32)[:]
    blkall_v = sbuf("blkall_s", [128, B * 4, 128], BF16)[:]
    idn_s = sbuf("idn_s", [128, 128], BF16)
    idnf_s = sbuf("idnf_s", [128, 128], F32)
    onesr_s = sbuf("onesr_s", [1, 128], BF16)
    onesc_s = sbuf("onesc_s", [128, 1], BF16)
    gb9_s = sbuf("gb9_s", [128, E + 1], F32)
    expb_s = sbuf("expb_s", [128, E], F32)
    iota8_s = sbuf("iota8_s", [128, E], F32)
    iota16_s = sbuf("iota16_s", [128, 16], F32)
    selg_s = sbuf("selg_s", [128, H], F32)
    selb_s = sbuf("selb_s", [128, H], F32)
    slng_s = sbuf("slng_s", [128, 4], F32)
    slnb_s = sbuf("slnb_s", [128, 4], F32)
    slnb_bf = sbuf("slnb_bf", [128, 4], BF16)
    sb1_s = sbuf("sb1_s", [1, H], BF16)
    sb2_s = sbuf("sb2_s", [1, H], BF16)
    bfin_s = sbuf("bfin_s", [1, D], BF16)
    bsh_bf = sbuf("bsh_bf", [1, D], BF16)
    bo_bf = sbuf("bo_bf", [1, D], BF16)
    xst_s = sbuf("xst_s", [128, 5, D], F32)
    sb3_f = xst_s[0:1, 3, :]
    bo_f = xst_s[0:1, 4, :]
    xmoe_v = xst_s[:, 2:4, :]
    xnt_s = sbuf("xnt_s", [128, 2, D], BF16)
    xnf_t = sbuf("xnf_t", [128, 2, 8, 128], BF16)
    mnri_s = sbuf("mnri_s", [128, 2, TPB], F32)
    kexp_s = sbuf("kexp_s", [128, 2, HID], BF16)
    vtok_s = sbuf("vtok_s", [128, 2, HID], BF16)
    qe32_s = sbuf("qe32_s", [128, 1, HID], F32)
    silu_s = sbuf("silu_s", [128, 1, H], F32)
    xx_s = sbuf("xx_s", [128, HID], F32)
    xy_s = sbuf("xy_s", [128, HID], F32)
    ksr_s = silu_s[0:1, 0, :].unsqueeze(0).squeeze(0).unsqueeze(0) if False else silu_s[0:1, 0, :]
    gate_s = sbuf("gate_s", [128, 2, H], F32)
    cacc_v = gate_s[:]
    brow_f = gate_s[0:1].rearrange("o a b -> o (a b)")
    qn_s = sbuf("qn_s", [128, 2, HID], BF16)
    sh_s = sbuf("sh_s", [128, 2, H], BF16)
    qf_s = sbuf("qf_s", [128, 2, 4, 128], BF16)
    shf_s = sbuf("shf_s", [128, 2, 4, 128], BF16)
    stats_s = sbuf("stats_s", [128, 12], F32)
    mv_s = sbuf("mv_s", [128, 2], F32)
    t0_s = sbuf("t0_s", [128, 1], F32)
    t1_s = sbuf("t1_s", [128, 1], F32)
    t2_s = sbuf("t2_s", [128, 1], F32)
    rinv_s = sbuf("rinv_s", [128, 1], F32)
    rs_s = sbuf("rs_s", [128, E], F32)
    blkt_s = sbuf("blkt_s", [128, 4, 128], BF16)
    blk_s = sbuf("blk_s", [128, 4, 128], BF16)

    ks4_s = sbuf("ks4_s", [128, 128], F32)
    invk_s = sbuf("invk_s", [128, 4], F32)
    xnmf_s = sbuf("xnmf_s", [128, 8, MS], BF16)
    idxf_s = sbuf("idxf_s", [128, MT], F32)
    idxi_s = sbuf("idxi_s", [128, MT], I32)
    scr_s = sbuf("scr_s", [128, MT, E + 1], F32)
    spr_s = sbuf("spr_s", [128, MT, E], F32)
    m8_s = sbuf("m8_s", [128, E], F32)
    ai_s = sbuf("ai_s", [128, E], U32)
    sel_s = sbuf("sel_s", [128, MT, E], BF16)
    self32_s = sbuf("self32_s", [128, MT, E], F32)
    selt_s = sbuf("selt_s", [E, MT, 128], BF16)
    wsh_s = sbuf("wsh_s", [128, MT], F32)
    wex_s = sbuf("wex_s", [128, MT], F32)
    maskb_s = sbuf("maskb_s", [128, MT, 16], F32)
    partial_s = sbuf("partial_s", [128, MT, D], F32)

    pb = [psum(f"pb{i}", [128, 512], F32) for i in range(6)]
    pt = [psum(f"pt{i}", [128, 1024], BF16) for i in range(2)]

    sems = {}
    for s in ["sx", "xa", "xb", "xc", "xd", "xe", "oa", "ob", "oc", "od", "oe", "of",
              "sw", "wa", "wb", "wc", "sk", "sm", "sl", "pe", "ve", "se", "ge", "cc"]:
        sems[s] = es.enter_context(nc.semaphore(s))

    sch = Sched()
    V = sch.val

    def dma(queue, out, in_, waits=(), sem=None):
        sem = sem or {"sync": "sx", "gpsimd": "sw", "scalar": "sx"}[queue]
        return sch.emit(queue, (lambda o, i: (lambda e: e.dma_start(out=o, in_=i)))(out, in_),
                        waits=waits, inc=sem, amt=16, drain=False)

    def op(eng, fn, waits=(), inc=False, drain=True):
        incmap = {"vector": "ve", "scalar": "se", "tensor": "pe", "gpsimd": "ge"}
        return sch.emit(eng, fn, waits=waits, inc=(incmap[eng] if inc else None),
                        drain=drain)

    def mark(eng):
        incmap = {"vector": "ve", "scalar": "se", "tensor": "pe", "gpsimd": "ge"}
        sem = incmap[eng]
        ops = sch.ops[eng]
        if ops and ops[-1][2] is None:
            fn, ws, _, _, dr = ops[-1]
            ops[-1] = (fn, ws, sem, 1, dr)
            sch.cnt[sem] = sch.cnt.get(sem, 0) + 1
            return sch.cnt[sem]
        return sch.emit(eng, lambda e: None, inc=sem)

    for out_s, in_d in [(idn_s[:], didn[:, :]), (idnf_s[:], didnf[:, :]),
                        (onesr_s[:], donesr[:, :]), (onesc_s[:], donesc[:, :]),
                        (gb9_s[:], dgb9[:, :]), (expb_s[:], dexpb[:, :]),
                        (iota8_s[:], diota8[:, :]), (iota16_s[:], diota16[:, :]),
                        (selg_s[:], dselg[:, :]),
                        (selb_s[:], dselb[:, :]), (slng_s[:], dslng[:, :]),
                        (slnb_s[:], dslnb[:, :]), (sb3_f, dsb3[:, :]),
                        (bo_f, dbo[:, :]),
                        (idxi_s[:], didb[:, :])]:
        dma("sync", out_s, in_d)
    n_const = V("sx")
    sw3_ld = dma("scalar", sw3stage,
                 dsw3t.rearrange("(k p) c -> p k c", p=128)[:, 0:2, :], sem="sk")

    def wload(out_s, in_d):
        dma("gpsimd", out_s, in_d.rearrange("(k p) c -> p k c", p=128))

    dma("gpsimd", wk_s[:], dwk.rearrange("(k p) c -> p k c", p=128), sem="wc")
    kv_w = dma("gpsimd", wv_s[:], dwv.rearrange("(k p) c -> p k c", p=128), sem="wc")
    wload(wq_s[:], dwq)
    wload(wo_s[:], dwo)
    wload(sw1t_s[:], dsw1t)
    wload(sw2t_s[:], dsw2t)
    wload(gsw_s[:], dgsw)
    dma("gpsimd", eb3_s[:], deb3[:, :])
    dma("gpsimd", sb1_s[:], dsb1[:, :])
    dma("gpsimd", sb2_s[:], dsb2[:, :])
    w_main = V("sw")

    op("vector", lambda e: e.memset(ks4_s[:], 0.0))
    for k in range(2):
        op("vector", (lambda kk: (lambda e: e.tensor_scalar_mul(
            sw3g_s[:, kk, :], sw3stage[:, kk, :], slng_s[:, kk:kk + 1])))(k),
            waits=[("sk", sw3_ld), ("sx", n_const)])
        op("vector", (lambda kk: (lambda e: e.tensor_copy(
            g_s[:, kk, :], sw3stage[:, kk, :])))(k))
    v_fold1 = mark("vector")
    sw3_ld2 = dma("scalar", sw3stage,
                  dsw3t.rearrange("(k p) c -> p k c", p=128)[:, 2:4, :],
                  waits=[("ve", v_fold1)], sem="sk")
    for k in range(2):
        op("vector", (lambda kk: (lambda e: e.tensor_scalar_mul(
            sw3g_s[:, kk + 2, :], sw3stage[:, kk, :], slng_s[:, kk + 2:kk + 3])))(k),
            waits=[("sk", sw3_ld2)])
        op("vector", (lambda kk: (lambda e: e.tensor_copy(
            g_s[:, kk + 2, :], sw3stage[:, kk, :])))(k))
    op("vector", lambda e: e.tensor_copy(slnb_bf[:], slnb_s[:]))
    op("vector", lambda e: e.tensor_copy(bo_bf[:], bo_f))
    v_sw3 = mark("vector")
    for k in range(4):
        for half in range(2):
            op("tensor", (lambda kk, hh: (lambda e: e.matmul(
                pb[hh][0:1, :], slnb_bf[:, kk:kk + 1],
                g_s[:, kk, hh * 512:(hh + 1) * 512],
                start=(kk == 0), stop=(kk == 3), skip_group_check=True)))(k, half),
                waits=([("ve", v_sw3)] if (k == 0 and half == 0) else []))
    pe_brow = mark("tensor")
    op("vector", lambda e: e.tensor_add(brow_f[0:1, 0:512], pb[0][0:1, :], sb3_f[0:1, 0:512]),
       waits=[("pe", pe_brow)])
    op("vector", lambda e: e.tensor_add(brow_f[0:1, 512:1024], pb[1][0:1, :], sb3_f[0:1, 512:1024]))
    op("vector", lambda e: e.tensor_copy(bsh_bf[0:1, :], brow_f[0:1, :]))
    op("vector", lambda e: e.tensor_add(brow_f[0:1, :], brow_f[0:1, :], bo_f))
    op("vector", lambda e: e.tensor_copy(bfin_s[0:1, :], brow_f[0:1, :]))
    v_bfin = mark("vector")

    def ln_stats(src_ap, fd, slot_mean, slot_rinv, waits, preload=None):
        if fd > 512:
            op("vector", lambda e: e.bn_stats(stats_s[:, 0:6], src_ap[:, 0:fd // 2]),
               waits=waits, drain=False)
            op("vector", lambda e: e.bn_stats(stats_s[:, 6:12], src_ap[:, fd // 2:fd]),
               drain=False)
            op("vector", lambda e: e.bn_aggr(mv_s[:], stats_s[:]))
        else:
            op("vector", (lambda s: (lambda e: e.bn_stats(stats_s[:, 0:6], s)))(src_ap),
               waits=waits, drain=False)
            op("vector", lambda e: e.bn_aggr(mv_s[:], stats_s[:, 0:6]))
        op("vector", lambda e: e.tensor_scalar_add(t0_s[:], mv_s[:, 1:2], EPS))
        vv = mark("vector")
        op("scalar", lambda e: e.sqrt(t1_s[:], t0_s[:]), waits=[("ve", vv)], drain=False)
        ss = mark("scalar")
        if False and preload is not None:
            op("scalar", (lambda pf: (lambda e: e.preload_act_table(pf)))(preload),
               drain=True)
        op("vector", (lambda sr: (lambda e: e.reciprocal(sr, t1_s[:])))(slot_rinv),
            waits=[("se", ss)], drain=False)
        if slot_mean is not None:
            op("vector", (lambda sm: (lambda e: e.tensor_copy(sm, mv_s[:, 0:1])))(slot_mean),
               drain=False)
        return mark("vector")

    tevac = {0: (None, 0), 1: (None, 0)}
    tsb = [0]

    def transpose_set(src_fn, nchunk, dst_ap, waits_pe, ev_eng="vector"):
        bank_id = tsb[0] % 2
        tsb[0] += 1
        bank = pt[bank_id]
        for c in range(nchunk):
            w = list(waits_pe) if c == 0 else []
            if c == 0 and tevac[bank_id][0] is not None:
                w.append(tevac[bank_id])
            op("tensor", (lambda cc: (lambda e: e.transpose(
                bank[:, cc * 128:(cc + 1) * 128], src_fn(cc), idn_s[:])))(c),
                waits=w)
        pe_n = mark("tensor")
        if ev_eng == "vector":
            op("vector", (lambda d, nch: (lambda e: e.tensor_copy(
                d, bank[:, 0:nch * 128].rearrange("p (c x) -> p c x", x=128))))(dst_ap, nchunk),
                waits=[("pe", pe_n)], drain=False)
            tevac[bank_id] = ("ve", mark("vector"))
            return V("ve"), V("se")
        else:
            op("scalar", (lambda d, nch: (lambda e: e.copy(
                d, bank[:, 0:nch * 128].rearrange("p (c x) -> p c x", x=128))))(dst_ap, nchunk),
                waits=[("pe", pe_n)], drain=False)
            tevac[bank_id] = ("se", mark("scalar"))
            return V("ve"), V("se")

    def apply_ln_tile(slot, t, waits=()):
        op("vector", (lambda sl, s3, tt: (lambda e: e.tensor_scalar(
            xnt_s[:, sl, :], xst_s[:, s3, :],
            mnri_s[:, 0, tt:tt + 1], mnri_s[:, 1, tt:tt + 1],
            ALU.subtract, ALU.mult)))(slot, t % 5, t), waits=waits)
        return mark("vector")

    out_dma = {}
    ve_x_done = {}
    s_kv_prev = 0
    pe_kst_prev = 0
    ks_ld_prev = 0
    cc_st_prev = 0
    xldA = {}
    xldB = {}
    evacsA = {}
    sB = {"s_sl": {}, "v_gate": {}, "v_out": {}, "v_sh": {}, "evF": {}, "evQ": {}}

    for b in range(BPC):
        def lastmod(tt):
            r = tt % 5
            return max(t2 for t2 in range(TPB) if t2 % 5 == r)

        def loadA(t, b=b):
            if t >= TPB:
                return
            s3 = t % 5
            r0 = b * N + t * 128
            w = [("ve", ve_x_done.get(("A", b, t - 5), 0))]
            if b == 0 and t in (3, 4):
                w.append(("ve", v_bfin))
            if b > 0:
                w.append(out_dma.get(("B", b - 1, lastmod(t)), ("oa", 0)))
            xsem = ["xa", "xb", "xc", "xd", "xe"][s3]
            xldA[(b, t)] = (xsem, dma("sync", xst_s[:, s3, :], dx[r0:r0 + 128, :],
                                      waits=w, sem=xsem))

        def lnA(t, b=b):
            s3 = t % 5
            loadA(t + 1)
            xsem, xv = xldA[(b, t)]
            ln_stats(xst_s[:, s3, :], D,
                     mnri_s[:, 0, t:t + 1], mnri_s[:, 1, t:t + 1], [(xsem, xv)])

        def applyTA(t, b=b):
            slot = t % 2
            v_xn = apply_ln_tile(slot, t)
            ve_x_done[("A", b, t)] = v_xn
            evacsA[(b, t)] = transpose_set(
                (lambda sl: (lambda c: xnt_s[:, sl, c * 128:(c + 1) * 128]))(slot), 8,
                xnf_t[:, slot, :, :], [("ve", v_xn)])

        def ctxA(t, b=b):
            slot = t % 2
            for p in range(4):
                op("tensor", (lambda pp, sl, tt: (lambda e: e.matmul(
                    pb[2 + pp][:, 0:128],
                    vtok_s[:, sl, pp * 128:(pp + 1) * 128],
                    kexp_s[:, sl, pp * 128:(pp + 1) * 128],
                    start=(tt == 0), stop=(tt == TPB - 1), skip_group_check=True)))(p, slot, t),
                    waits=([("se", sKVd[t])] if p == 0 else []))
                op("tensor", (lambda pp, sl, tt: (lambda e: e.matmul(
                    pb[2 + pp][0:1, 256:384], onesc_s[:],
                    kexp_s[:, sl, pp * 128:(pp + 1) * 128],
                    start=False, stop=(tt == TPB - 1), skip_group_check=True)))(p, slot, t))

        def backA(t, b=b):
            slot = t % 2
            v_evac, s_evac = evacsA[(b, t)]
            global_w = [("ve", v_evac), ("se", s_evac), ("se", sKV[0])]
            if b == 0 and t == 0:
                global_w.append(("wc", kv_w))
            for k in range(8):
                op("tensor", (lambda kk, sl: (lambda e: e.matmul(
                    pb[0][:], xnf_t[:, sl, kk, :], wk_s[:, kk, :],
                    start=(kk == 0), stop=(kk == 7), skip_group_check=True)))(k, slot),
                    waits=(global_w if k == 0 else []))
                op("tensor", (lambda kk, sl: (lambda e: e.matmul(
                    pb[1][:], xnf_t[:, sl, kk, :], wv_s[:, kk, :],
                    start=(kk == 0), stop=(kk == 7), skip_group_check=True)))(k, slot))
            pe_kv = mark("tensor")
            op("scalar", (lambda sl: (lambda e: e.activation(
                kexp_s[:, sl, :], pb[0][:], AF.Exp)))(slot), waits=[("pe", pe_kv)],
                drain=False)
            op("scalar", (lambda sl: (lambda e: e.copy(
                vtok_s[:, sl, :], pb[1][:])))(slot), drain=False)
            sKV[0] = mark("scalar")
            sKVd[t] = sKV[0]

        sKV = [s_kv_prev]
        sKVd = {}
        loadA(0)
        lnA(0)
        applyTA(0)
        for t in range(TPB):
            if t + 1 < TPB:
                lnA(t + 1)
            backA(t)
            if t - 1 >= 0:
                ctxA(t - 1)
            if t + 1 < TPB:
                applyTA(t + 1)
        ctxA(TPB - 1)
        s_kv_prev = sKV[0]
        pe_ctx = mark("tensor")

        if b == 0:
            op("vector", lambda e: e.memset(blkt_s[:], 0.0), waits=[("pe", pe_ctx)])
        for p in range(4):
            op("vector", (lambda pp: (lambda e: e.tensor_copy(
                blkt_s[0:64, pp, 0:64], pb[2 + pp][0:64, 0:64])))(p),
                waits=[("pe", pe_ctx), ("sk", ks_ld_prev)])
            op("vector", (lambda pp: (lambda e: e.tensor_copy(
                blkt_s[64:128, pp, 64:128], pb[2 + pp][64:128, 64:128])))(p))
            op("vector", (lambda pp: (lambda e: e.tensor_copy(
                ksr_s[0:1, pp * 128:(pp + 1) * 128], pb[2 + pp][0:1, 256:384])))(p))
        v_blkt = mark("vector")
        ks_ld = dma("sync", ks4_s[0:4, 0:128], ksr_s[0:1, :],
                    waits=[("ve", v_blkt), ("pe", pe_kst_prev)], sem="sk")
        ks_ld_prev = ks_ld
        pe_kst = op("tensor", lambda e: e.transpose(
            pb[0][:, 0:128], ks4_s[:], idnf_s[:]),
            waits=[("sk", ks_ld), ("se", V("se"))], inc=True)
        pe_kst_prev = pe_kst
        op("vector", lambda e: e.reciprocal(invk_s[:], pb[0][:, 0:4]), waits=[("pe", pe_kst)])
        op("vector", lambda e: e.tensor_scalar_mul(invk_s[:], invk_s[:], SCALE))
        v_invk = mark("vector")

        v_g = v_invk
        for p in range(4):
            for half in range(2):
                op("tensor", (lambda pp, hh: (lambda e: e.matmul(
                    pb[2 + hh][:], blkt_s[:, pp, :], wo_s[:, pp, hh * 512:(hh + 1) * 512],
                    start=True, stop=True, skip_group_check=True)))(p, half),
                    waits=(([("ve", v_g)] + ([("sw", w_main)] if b == 0 and p == 0 else []))
                           if half == 0 else []))
            pe_g = mark("tensor")
            op("vector", (lambda pp: (lambda e: e.tensor_scalar_mul(
                g_s[:, pp, 0:512], pb[2][:], invk_s[:, pp:pp + 1])))(p),
                waits=[("pe", pe_g)])
            op("vector", (lambda pp: (lambda e: e.tensor_scalar_mul(
                g_s[:, pp, 512:1024], pb[3][:], invk_s[:, pp:pp + 1])))(p))
            v_g = mark("vector")
            wbt = [("ve", v_g)]
            if tevac[0][0] is not None:
                wbt.append(tevac[0])
            pe_bt = op("tensor", (lambda pp: (lambda e: e.transpose(
                pt[0][:, 0:128], blkt_s[:, pp, :], idn_s[:])))(p),
                waits=wbt, inc=True)
            op("vector", (lambda pp: (lambda e: e.tensor_scalar_mul(
                blk_s[:, pp, :], pt[0][:, 0:128], invk_s[:, pp:pp + 1])))(p),
                waits=[("pe", pe_bt), ("sw", cc_st_prev)])
            tevac[0] = ("ve", mark("vector"))
        v_blk = tevac[0][1]
        cc_st = dma("gpsimd", cc_in[b:b + 1].rearrange("o q p c -> p (o q) c"), blk_s[:],
                    waits=[("ve", v_blk)])
        cc_st_prev = cc_st

        if b == BPC - 1:
            def do_cc(e):
                return e.collective_compute(
                    "AllGather", ALU.bypass,
                    replica_groups=[list(range(NCORES))],
                    ins=[cc_in.opt()],
                    outs=[cc_out.opt()])
            sch.emit("gpsimd", do_cc, waits=[("sw", V("sw"))], inc="cc", amt=1, drain=False)

        def loadB(t, b=b):
            if t >= TPB:
                return
            s3 = t % 5
            r0 = b * N + t * 128
            w = [out_dma.get(("B", b, t - 5), ("oa", 0)),
                 ("ve", ve_x_done.get(("A", b, lastmod(t)), 0))]
            if t < 5 and b > 0:
                w.append(out_dma.get(("B", b - 1, lastmod(t)), ("oa", 0)))
            xsem = ["xa", "xb", "xc", "xd", "xe"][s3]
            xldB[(b, t)] = (xsem, dma("sync", xst_s[:, s3, :], dx[r0:r0 + 128, :],
                                      waits=w, sem=xsem))

        def frontB(t, b=b):
            slot = t % 2
            loadB(t + 1)
            xsem, xv = xldB[(b, t)]
            v_xn = apply_ln_tile(slot, t, waits=[(xsem, xv)])
            ve_x_done[("B", b, t)] = v_xn
            sB["evF"][t] = transpose_set(
                (lambda sl: (lambda c: xnt_s[:, sl, c * 128:(c + 1) * 128]))(slot), 8,
                xnf_t[:, slot, :, :], [("ve", v_xn)])

        def backEarlyPE(t, b=b):
            slot = t % 2
            v_evac, s_evac = sB["evF"][t]
            wq_w = [("ve", v_evac), ("se", s_evac),
                    ("se", sB["s_sl"].get(t - 1, 0)), ("ve", sB["v_xx"].get(t - 1, 0))]
            if b == 0 and t == 0:
                wq_w.append(("sw", w_main))
            for k in range(8):
                op("tensor", (lambda kk, sl: (lambda e: e.matmul(
                    pb[0][:], xnf_t[:, sl, kk, :], wq_s[:, kk, :],
                    start=(kk == 0), stop=(kk == 7), skip_group_check=True)))(k, slot),
                    waits=(wq_w if k == 0 else []))
                op("tensor", (lambda kk, sl: (lambda e: e.matmul(
                    pb[1][:], xnf_t[:, sl, kk, :], sw1t_s[:, kk, :],
                    start=(kk == 0), stop=False, skip_group_check=True)))(k, slot))
                op("tensor", (lambda kk, sl: (lambda e: e.matmul(
                    pb[2][:], xnf_t[:, sl, kk, :], sw2t_s[:, kk, :],
                    start=(kk == 0), stop=False, skip_group_check=True)))(k, slot))
            op("tensor", lambda e: e.matmul(pb[1][:], onesr_s[:], sb1_s[:],
                                            start=False, stop=True, skip_group_check=True))
            op("tensor", lambda e: e.matmul(pb[2][:], onesr_s[:], sb2_s[:],
                                            start=False, stop=True, skip_group_check=True))
            sB["pe_qx"][t] = mark("tensor")

        def backEarlyVS(t, b=b):
            slot = t % 2
            pe_qx = sB["pe_qx"][t]
            op("scalar", lambda e: e.activation(
                qe32_s[:, 0, :], pb[0][:], AF.Exp), waits=[("pe", pe_qx)], drain=False)
            s_q = mark("scalar")
            op("scalar", lambda e: e.activation(
                silu_s[:, 0, :], pb[1][:], AF.Tanh, scale=0.5), drain=False)
            sB["s_sl"][t] = mark("scalar")
            op("vector", lambda e: e.tensor_copy(
                xx_s[:], pb[2][:]), waits=[("pe", pe_qx)], drain=False)
            op("vector", (lambda st: (lambda e: e.tensor_tensor(
                xy_s[:], xx_s[:], pb[1][:], ALU.mult)))(t),
                waits=[("se", sB["s_sl"][t])])
            sB["v_xx"][t] = mark("vector")

            op("vector", lambda e: e.reduce_sum(
                rs_s[:].unsqueeze(2), qe32_s[:, 0, :].rearrange("p (h d) -> p h d", d=DH),
                axis=AX.X), waits=[("se", s_q)], drain=False)
            op("vector", lambda e: e.reciprocal(rs_s[:], rs_s[:]))
            op("vector", (lambda sl: (lambda e: e.tensor_tensor(
                qn_s[:, sl, :].rearrange("p (h d) -> p h d", d=DH),
                qe32_s[:, 0, :].rearrange("p (h d) -> p h d", d=DH),
                rs_s[:].unsqueeze(2).broadcast_to([128, E, DH]), ALU.mult)))(slot))
            op("vector", lambda e: e.tensor_scalar(
                silu_s[:, 0, :], silu_s[:, 0, :], 0.5, 0.5, ALU.mult, ALU.add),
                waits=[("se", sB["s_sl"][t])], drain=False)
            op("vector", (lambda sl: (lambda e: e.tensor_tensor(
                gate_s[:, sl, :], silu_s[:, 0, :], xy_s[:], ALU.mult)))(slot))
            sB["v_gate"][t] = mark("vector")
            ln_stats(gate_s[:, slot, :], H, None, rinv_s[:], [], preload=AF.Exp)
            op("vector", (lambda sl: (lambda e: e.tensor_scalar(
                sh_s[:, sl, :], gate_s[:, sl, :], mv_s[:, 0:1], rinv_s[:, 0:1],
                ALU.subtract, ALU.mult)))(slot))
            sB["v_sh"][t] = mark("vector")

        def backLatePE(t, b=b):
            slot = t % 2
            transpose_set((lambda sl: (lambda c: qn_s[:, sl, c * 128:(c + 1) * 128]))(slot), 4,
                          qf_s[:, slot, :, :], [("ve", sB["v_sh"][t])], ev_eng="scalar")
            v_evac, s_evac = transpose_set(
                (lambda sl: (lambda c: sh_s[:, sl, c * 128:(c + 1) * 128]))(slot), 4,
                shf_s[:, slot, :, :], [], ev_eng="scalar")

            wcs = [("ve", v_evac), ("se", s_evac), ("ve", sB["v_out"].get(t - 1, 0))]
            first = True
            for p in range(4):
                op("tensor", (lambda pp, sl: (lambda e: e.matmul(
                    pb[3][:], qf_s[:, sl, pp, :], g_s[:, pp, 0:512],
                    start=(pp == 0), stop=False, skip_group_check=True)))(p, slot),
                    waits=(wcs if first else []))
                first = False
                op("tensor", (lambda pp, sl: (lambda e: e.matmul(
                    pb[4][:], qf_s[:, sl, pp, :], g_s[:, pp, 512:1024],
                    start=(pp == 0), stop=False, skip_group_check=True)))(p, slot))
            for p in range(4):
                op("tensor", (lambda pp, sl: (lambda e: e.matmul(
                    pb[3][:], shf_s[:, sl, pp, :], sw3g_s[:, pp, 0:512],
                    start=False, stop=False, skip_group_check=True)))(p, slot))
                op("tensor", (lambda pp, sl: (lambda e: e.matmul(
                    pb[4][:], shf_s[:, sl, pp, :], sw3g_s[:, pp, 512:1024],
                    start=False, stop=False, skip_group_check=True)))(p, slot))
            op("tensor", lambda e: e.matmul(pb[3][:], onesr_s[:], bfin_s[0:1, 0:512],
                                            start=False, stop=True, skip_group_check=True),
               waits=([("ve", v_bfin)] if (b == 0 and t == 0) else []))
            op("tensor", lambda e: e.matmul(pb[4][:], onesr_s[:], bfin_s[0:1, 512:1024],
                                            start=False, stop=True, skip_group_check=True))
            sB["pe_cs"][t] = mark("tensor")

        def backLateFin(t, b=b):
            pe_cs = sB["pe_cs"][t]
            r0 = b * N + t * 128
            s3 = t % 5
            op("vector", (lambda s3: (lambda e: e.tensor_add(
                xst_s[:, s3, 0:512], xst_s[:, s3, 0:512], pb[3][:])))(s3),
                waits=[("pe", pe_cs)], drain=False)
            op("vector", (lambda s3: (lambda e: e.tensor_add(
                xst_s[:, s3, 512:1024], xst_s[:, s3, 512:1024], pb[4][:])))(s3),
                drain=False)
            v_out = mark("vector")
            sB["v_out"][t] = v_out
            osem = ["oa", "ob", "od", "oe", "of"][s3]
            so = sch.emit("sync", (lambda s3, rr: (lambda e: e.dma_start(
                out=dom[rr:rr + 128, :], in_=xst_s[:, s3, :])))(s3, r0),
                waits=[("ve", v_out)], inc=osem, amt=16)
            out_dma[("B", b, t)] = (osem, so)

        sB = {"s_sl": {}, "v_gate": {}, "v_xx": {}, "v_out": {}, "v_sh": {},
              "evF": {}, "pe_cs": {}, "pe_qx": {}}
        loadB(0)
        for j in range(TPB + 3):
            if 0 <= j - 1 < TPB:
                backEarlyPE(j - 1)
            if j - 3 >= 0:
                backLatePE(j - 3)
            if j < TPB:
                frontB(j)
            if 0 <= j - 1 < TPB:
                backEarlyVS(j - 1)
            if j - 3 >= 0:
                backLateFin(j - 3)

    xm_ld = dma("sync", xmoe_v, dxmoe.rearrange("(m p) d -> p m d", p=128),
                waits=[out_dma[("B", BPC - 1, 27)],
                       out_dma[("B", BPC - 1, 28)]], sem="sm")
    blkall_ld = dma("sync", blkall_v, cc_out.rearrange("b q p c -> p (b q) c"),
                    waits=[("cc", 1), ("ve", v_bfin)], sem="sl")

    moe_w = [("sm", xm_ld), ("ve", V("ve")), ("se", V("se")), ("pe", V("pe"))]
    for mt in range(MT):
        ln_stats(xmoe_v[:, mt, :], D, t0_s[:], rinv_s[:], moe_w if mt == 0 else [])
        op("vector", (lambda m: (lambda e: e.tensor_scalar(
            xnt_s[:, m, :], xmoe_v[:, m, :], t0_s[:, 0:1], rinv_s[:, 0:1],
            ALU.subtract, ALU.mult)))(mt))
        v_xnm = mark("vector")
        transpose_set((lambda m: (lambda c: xnt_s[:, m, c * 128:(c + 1) * 128]))(mt), 8,
                      xnmf_s[:, :, mt * 128:(mt + 1) * 128], [("ve", v_xnm)])
    v_evac, s_evac = V("ve"), V("se")

    op("vector", lambda e: e.tensor_copy(idxf_s[:], idxi_s[:]), waits=[("sx", n_const)])
    v_idx = mark("vector")

    v_scr_prev = 0
    s_q_prev = 0
    for mt in range(MT):
        for k in range(8):
            op("tensor", (lambda kk, m: (lambda e: e.matmul(
                pb[0][:, 0:E + 1], xnmf_s[:, kk, m * 128:(m + 1) * 128], gsw_s[:, kk, :],
                start=(kk == 0), stop=(kk == 7), skip_group_check=True)))(k, mt),
                waits=(([("ve", v_evac), ("se", s_evac)] if mt == 0
                        else [("ve", v_scr_prev)]) if k == 0 else []))
        pe_sc = mark("tensor")
        op("vector", (lambda m: (lambda e: e.tensor_add(
            scr_s[:, m, :], pb[0][:, 0:E + 1], gb9_s[:])))(mt), waits=[("pe", pe_sc)])
        v_scr_prev = mark("vector")
        op("scalar", (lambda m: (lambda e: e.activation(
            scr_s[:, m, :], scr_s[:, m, :], AF.Tanh, scale=0.5)))(mt),
            waits=[("ve", v_scr_prev)])
        s_scr = mark("scalar")
        op("vector", (lambda m: (lambda e: e.tensor_scalar(
            scr_s[:, m, :], scr_s[:, m, :], 0.5, 0.5, ALU.mult, ALU.add)))(mt),
            waits=[("se", s_scr)])
        s_scr_v = mark("vector")
        op("vector", (lambda m: (lambda e: e.tensor_add(
            spr_s[:, m, :], scr_s[:, m, 0:E], expb_s[:])))(mt))
        op("vector", (lambda m: (lambda e: e.max(m8_s[:], spr_s[:, m, :])))(mt))
        op("vector", (lambda m: (lambda e: e.max_index(ai_s[:], m8_s[:], spr_s[:, m, :])))(mt))
        op("vector", lambda e: e.tensor_copy(t0_s[:], ai_s[:, 0:1]))
        op("vector", (lambda m: (lambda e: e.tensor_single_scalar(
            self32_s[:, m, :], iota8_s[:], t0_s[:, 0:1], ALU.is_equal)))(mt))
        op("vector", (lambda m: (lambda e: e.tensor_copy(sel_s[:, m, :], self32_s[:, m, :])))(mt))
        op("vector", (lambda m: (lambda e: e.tensor_tensor(
            spr_s[:, m, :], scr_s[:, m, 0:E], self32_s[:, m, :], ALU.mult)))(mt))
        op("vector", (lambda m: (lambda e: e.reduce_sum(
            t1_s[:], spr_s[:, m, :], axis=AX.X)))(mt))
        op("vector", (lambda m: (lambda e: e.tensor_tensor(
            t0_s[:], scr_s[:, m, E:E + 1], t1_s[:], ALU.max)))(mt))
        op("vector", (lambda m: (lambda e: e.tensor_tensor(
            t2_s[:], scr_s[:, m, E:E + 1], t0_s[:], ALU.subtract)))(mt))
        op("vector", lambda e: e.tensor_sub(t1_s[:], t1_s[:], t0_s[:]))
        v_pre = mark("vector")
        op("scalar", lambda e: e.activation(t2_s[:], t2_s[:], AF.Exp), waits=[("ve", v_pre)])
        op("scalar", lambda e: e.activation(t1_s[:], t1_s[:], AF.Exp))
        s_ex = mark("scalar")
        op("vector", lambda e: e.tensor_add(t0_s[:], t1_s[:], t2_s[:]), waits=[("se", s_ex)])
        op("vector", lambda e: e.reciprocal(t0_s[:], t0_s[:]))
        op("vector", (lambda m: (lambda e: e.tensor_mul(wsh_s[:, m:m + 1], t2_s[:], t0_s[:])))(mt))
        op("vector", (lambda m: (lambda e: e.tensor_mul(wex_s[:, m:m + 1], t1_s[:], t0_s[:])))(mt))
        v_scr_prev = mark("vector")

    v_qn_prev = 0
    for mt in range(MT):
        for k in range(8):
            op("tensor", (lambda kk, m: (lambda e: e.matmul(
                pb[0][:], xnmf_s[:, kk, m * 128:(m + 1) * 128], wq_s[:, kk, :],
                start=(kk == 0), stop=(kk == 7), skip_group_check=True)))(k, mt),
                waits=(([("ve", v_scr_prev)] if mt == 0
                        else [("se", s_q_prev), ("ve", v_qn_prev)])
                       if k == 0 else []))
        pe_q = mark("tensor")
        op("scalar", lambda e: e.activation(
            qe32_s[:, 0, :], pb[0][:], AF.Exp), waits=[("pe", pe_q)])
        s_q_prev = mark("scalar")
        op("vector", lambda e: e.reduce_sum(
            rs_s[:].unsqueeze(2), qe32_s[:, 0, :].rearrange("p (h d) -> p h d", d=DH),
            axis=AX.X), waits=[("se", s_q_prev)])
        op("vector", lambda e: e.reciprocal(rs_s[:], rs_s[:]))
        op("vector", (lambda m: (lambda e: e.tensor_tensor(
            qn_s[:, m, :].rearrange("p (h d) -> p h d", d=DH),
            qe32_s[:, 0, :].rearrange("p (h d) -> p h d", d=DH),
            rs_s[:].unsqueeze(2).broadcast_to([128, E, DH]), ALU.mult)))(mt))
        v_qn = mark("vector")
        v_qn_prev = v_qn
        transpose_set((lambda m: (lambda c: qn_s[:, m, c * 128:(c + 1) * 128]))(mt), 4,
                      qf_s[:, mt, :, :], [("ve", v_qn)])
    v_evac, s_evac = V("ve"), V("se")

    for mt in range(MT):
        op("vector", (lambda m: (lambda e: e.memset(cacc_v[:, m, :], 0.0)))(mt))
        op("vector", (lambda m: (lambda e: e.tensor_single_scalar(
            maskb_s[:, m, :], iota16_s[:], idxf_s[:, m:m + 1], ALU.is_equal)))(mt),
            waits=[("ve", v_idx)] if mt == 0 else [])
    v_cz = mark("vector")
    prev_cb = {0: 0, 1: 0}
    last_cb = {}
    for bb in range(B):
        for mt in range(MT):
            i2 = (bb * MT + mt) % 2
            wcb = ([("ve", v_cz), ("se", s_evac), ("sl", blkall_ld)]
                   if (bb == 0 and mt == 0) else [("ve", prev_cb[i2])])
            for p in range(4):
                op("tensor", (lambda m, bbb, pp, bk: (lambda e: e.matmul(
                    pb[bk][:, pp * 128:(pp + 1) * 128], qf_s[:, m, pp, :],
                    blkall_v[:, bbb * 4 + pp, :],
                    start=(pp == 0), stop=(pp == 3), skip_group_check=True)))(mt, bb, p, i2),
                    waits=(wcb if p == 0 else []))
            pe_cb = mark("tensor")
            op("vector", (lambda m, bbb, bk: (lambda e: e.scalar_tensor_tensor(
                cacc_v[:, m, :], pb[bk][:], maskb_s[:, m, bbb:bbb + 1], cacc_v[:, m, :],
                ALU.mult, ALU.add)))(mt, bb, i2), waits=[("pe", pe_cb)])
            prev_cb[i2] = mark("vector")
            last_cb[mt] = prev_cb[i2]

    for mt in range(MT):
        op("vector", (lambda m: (lambda e: e.tensor_copy(
            qn_s[:, m, :], cacc_v[:, m, :])))(mt), waits=[("ve", last_cb[MT - 1])])
        v_cb = mark("vector")
        v_evac, s_evac = transpose_set(
            (lambda m: (lambda c: qn_s[:, m, c * 128:(c + 1) * 128]))(mt), 4,
            qf_s[:, mt, :, :], [("ve", v_cb)])
        for p in range(4):
            op("tensor", (lambda m, pp: (lambda e: e.matmul(
                pb[3][:], qf_s[:, m, pp, :], wo_s[:, pp, 0:512],
                start=(pp == 0), stop=False, skip_group_check=True)))(mt, p),
                waits=([("ve", v_evac), ("se", s_evac)] if p == 0 else []))
            op("tensor", (lambda m, pp: (lambda e: e.matmul(
                pb[4][:], qf_s[:, m, pp, :], wo_s[:, pp, 512:1024],
                start=(pp == 0), stop=False, skip_group_check=True)))(mt, p))
        op("tensor", lambda e: e.matmul(pb[3][:], onesr_s[:], bo_bf[0:1, 0:512],
                                        start=False, stop=True, skip_group_check=True))
        op("tensor", lambda e: e.matmul(pb[4][:], onesr_s[:], bo_bf[0:1, 512:1024],
                                        start=False, stop=True, skip_group_check=True))
        mark("tensor")

        for k in range(8):
            op("tensor", (lambda kk, m: (lambda e: e.matmul(
                pb[1][:], xnmf_s[:, kk, m * 128:(m + 1) * 128], sw1t_s[:, kk, :],
                start=(kk == 0), stop=False, skip_group_check=True)))(k, mt),
                waits=([("ve", V("ve"))] if (k == 0 and mt == 1) else []))
            op("tensor", (lambda kk, m: (lambda e: e.matmul(
                pb[2][:], xnmf_s[:, kk, m * 128:(m + 1) * 128], sw2t_s[:, kk, :],
                start=(kk == 0), stop=False, skip_group_check=True)))(k, mt))
        op("tensor", lambda e: e.matmul(pb[1][:], onesr_s[:], sb1_s[:],
                                        start=False, stop=True, skip_group_check=True))
        op("tensor", lambda e: e.matmul(pb[2][:], onesr_s[:], sb2_s[:],
                                        start=False, stop=True, skip_group_check=True))
        pe_x12 = mark("tensor")
        op("scalar", lambda e: e.activation(
            silu_s[:, 0, :], pb[1][:], AF.Tanh, scale=0.5), waits=[("pe", pe_x12)])
        s_sl = mark("scalar")
        op("vector", lambda e: e.tensor_scalar(
            silu_s[:, 0, :], silu_s[:, 0, :], 0.5, 0.5, ALU.mult, ALU.add),
            waits=[("se", s_sl)])
        op("vector", (lambda m: (lambda e: e.tensor_tensor(
            gate_s[:, m, :], silu_s[:, 0, :], pb[1][:], ALU.mult)))(mt))
        op("vector", (lambda m: (lambda e: e.tensor_tensor(
            gate_s[:, m, :], gate_s[:, m, :], pb[2][:], ALU.mult)))(mt))
        ln_stats(gate_s[:, mt, :], H, t0_s[:], rinv_s[:], [])
        op("vector", (lambda m: (lambda e: e.tensor_scalar(
            sh_s[:, m, :], gate_s[:, m, :], t0_s[:, 0:1], rinv_s[:, 0:1],
            ALU.subtract, ALU.mult)))(mt))
        v_sh = mark("vector")
        v_evac2, s_evac2 = transpose_set(
            (lambda m: (lambda c: sh_s[:, m, c * 128:(c + 1) * 128]))(mt), 4,
            shf_s[:, mt, :, :], [("ve", v_sh)])
        for p in range(4):
            op("tensor", (lambda m, pp: (lambda e: e.matmul(
                pb[2][:], shf_s[:, m, pp, :], sw3g_s[:, pp, 0:512],
                start=(pp == 0), stop=False, skip_group_check=True)))(mt, p),
                waits=([("ve", v_evac2), ("se", s_evac2)] if p == 0 else []))
            op("tensor", (lambda m, pp: (lambda e: e.matmul(
                pb[5][:], shf_s[:, m, pp, :], sw3g_s[:, pp, 512:1024],
                start=(pp == 0), stop=False, skip_group_check=True)))(mt, p))
        op("tensor", lambda e: e.matmul(pb[2][:], onesr_s[:], bsh_bf[0:1, 0:512],
                                        start=False, stop=True, skip_group_check=True))
        op("tensor", lambda e: e.matmul(pb[5][:], onesr_s[:], bsh_bf[0:1, 512:1024],
                                        start=False, stop=True, skip_group_check=True))
        pe_sh = mark("tensor")

        op("vector", (lambda m: (lambda e: e.scalar_tensor_tensor(
            partial_s[:, m, 0:512], pb[2][:], wsh_s[:, m:m + 1], xmoe_v[:, m, 0:512],
            ALU.mult, ALU.add)))(mt), waits=[("pe", pe_sh)])
        op("vector", (lambda m: (lambda e: e.scalar_tensor_tensor(
            partial_s[:, m, 512:1024], pb[5][:], wsh_s[:, m:m + 1], xmoe_v[:, m, 512:1024],
            ALU.mult, ALU.add)))(mt))
        op("vector", (lambda m: (lambda e: e.tensor_add(
            partial_s[:, m, 0:512], partial_s[:, m, 0:512], pb[3][:])))(mt))
        op("vector", (lambda m: (lambda e: e.tensor_add(
            partial_s[:, m, 512:1024], partial_s[:, m, 512:1024], pb[4][:])))(mt))
    v_part = mark("vector")

    for mt in range(MT):
        wst = [("ve", v_part)]
        if tevac[0][0] is not None:
            wst.append(tevac[0])
        pe_st = op("tensor", (lambda m: (lambda e: e.transpose(
            pt[0][0:E, 0:128], sel_s[:, m, :], idn_s[:])))(mt),
            waits=wst, inc=True)
        op("vector", (lambda m: (lambda e: e.tensor_copy(
            selt_s[0:E, m, :], pt[0][0:E, 0:128])))(mt), waits=[("pe", pe_st)])
        tevac[0] = ("ve", mark("vector"))
    v_selt = tevac[0][1]

    hpair = {0: (pb[0], pb[1]), 1: (pb[2], pb[3])}
    NI = E * MT
    st_eb = {}

    def ew1_load(e_i):
        buf = e_i % 2
        wsem = "wa" if buf == 0 else "wb"
        w = ([] if e_i < 2 else [("pe", st_eb.get(("peh", e_i - 2), 0))])
        dma("gpsimd", ew1_s[:, buf, :, :],
            dew1t[e_i:e_i + 1].rearrange("o (k p) c -> p (o k) c", p=128), waits=w, sem=wsem)
        dma("gpsimd", ebrow_s[0:1, buf, 0, :], deb1[e_i:e_i + 1, :], sem=wsem)
        st_eb[("d1", e_i)] = (wsem, dma(
            "gpsimd", ebrow_s[0:1, buf, 1, :], deb2[e_i:e_i + 1, :], sem=wsem))

    def ew2_load(e_i):
        wsem = "wa" if (e_i % 2) == 0 else "wb"
        w = ([] if e_i == 0 else [("pe", st_eb[("peh_i", e_i * MT - 1)])])
        st_eb[("d2", e_i)] = (wsem, dma(
            "gpsimd", ew2_s[:, 0, :, :],
            dew2t[e_i:e_i + 1].rearrange("o (k p) c -> p (o k) c", p=128),
            waits=w, sem=wsem))

    def ew3_load(e_i):
        w = ([] if e_i == 0 else [("pe", st_eb[("pez", e_i - 1, MT - 1)])])
        st_eb[("d3", e_i)] = ("wc", dma(
            "gpsimd", ew3_s[:, 0, :, :],
            dew3t[e_i:e_i + 1].rearrange("o (k p) c -> p (o k) c", p=128),
            waits=w, sem="wc"))

    def s1_h(i):
        e_i, mt = divmod(i, MT)
        buf = e_i % 2
        ph1, ph2 = hpair[i % 2]
        wsem = "wa" if buf == 0 else "wb"
        wh = [st_eb[("d2", e_i)],
              ("ve", st_eb.get(("vgate", i - 2), 0)),
              ("se", st_eb.get(("setanh", i - 2), 0))]
        if i == 0:
            wh += [("ve", V("ve")), ("se", V("se"))]
        for k in range(8):
            op("tensor", (lambda kk, m, bf, p1: (lambda e: e.matmul(
                p1[:], xnmf_s[:, kk, m * 128:(m + 1) * 128], ew1_s[:, bf, kk, :],
                start=(kk == 0), stop=False, skip_group_check=True)))(k, mt, buf, ph1),
                waits=(wh if k == 0 else []))
            op("tensor", (lambda kk, m, p2: (lambda e: e.matmul(
                p2[:], xnmf_s[:, kk, m * 128:(m + 1) * 128], ew2_s[:, 0, kk, :],
                start=(kk == 0), stop=False, skip_group_check=True)))(k, mt, ph2))
        op("tensor", (lambda bf, p1: (lambda e: e.matmul(
            p1[:], onesr_s[:], ebrow_s[0:1, bf, 0, :],
            start=False, stop=True, skip_group_check=True)))(buf, ph1))
        op("tensor", (lambda bf, p2: (lambda e: e.matmul(
            p2[:], onesr_s[:], ebrow_s[0:1, bf, 1, :],
            start=False, stop=True, skip_group_check=True)))(buf, ph2))
        st_eb[("peh_i", i)] = mark("tensor")
        e_done = (mt == MT - 1)
        if e_done:
            st_eb[("peh", e_i)] = st_eb[("peh_i", i)]

    def s2_vs(i):
        e_i, mt = divmod(i, MT)
        ph1, ph2 = hpair[i % 2]
        op("scalar", (lambda p1: (lambda e: e.activation(
            silu_s[:, 0, :], p1[:], AF.Tanh, scale=0.5)))(ph1),
            waits=[("pe", st_eb[("peh_i", i)])], drain=False)
        st_eb[("setanh", i)] = mark("scalar")
        op("vector", (lambda m, p1: (lambda e: e.scalar_tensor_tensor(
            gate_s[:, m, :], silu_s[:, 0, :], 1.0, p1[:], ALU.add, ALU.mult)))(mt, ph1),
            waits=[("se", st_eb[("setanh", i)])], drain=False)
        op("vector", (lambda m, p2: (lambda e: e.scalar_tensor_tensor(
            gate_s[:, m, :], gate_s[:, m, :], 0.5, p2[:], ALU.mult, ALU.mult)))(mt, ph2))
        st_eb[("vgate", i)] = mark("vector")
        ln_stats(gate_s[:, mt, :], H, None, rinv_s[:], [])
        op("vector", (lambda m, ee: (lambda e: e.tensor_mul(
            t2_s[:], rinv_s[:], self32_s[:, m, ee:ee + 1])))(mt, e_i))
        op("vector", (lambda m: (lambda e: e.tensor_scalar(
            gate_s[:, m, :], gate_s[:, m, :], mv_s[:, 0:1], t2_s[:, 0:1],
            ALU.subtract, ALU.mult)))(mt))
        op("vector", (lambda m: (lambda e: e.tensor_tensor(
            gate_s[:, m, :], gate_s[:, m, :], selg_s[:], ALU.mult)))(mt))
        op("vector", (lambda m, ee: (lambda e: e.scalar_tensor_tensor(
            sh_s[:, m, :], selb_s[:], self32_s[:, m, ee:ee + 1], gate_s[:, m, :],
            ALU.mult, ALU.add)))(mt, e_i))
        st_eb[("vz", i)] = mark("vector")

    def s3_z(i):
        e_i, mt = divmod(i, MT)
        transpose_set((lambda m: (lambda c: sh_s[:, m, c * 128:(c + 1) * 128]))(mt), 4,
                      shf_s[:, mt, :, :], [("ve", st_eb[("vz", i)])], ev_eng="scalar")
        vv, ss = V("ve"), V("se")
        wz = [("se", ss), ("ve", st_eb.get(("vstt", i - 1), 0)),
              st_eb[("d3", e_i)]]
        for p in range(4):
            op("tensor", (lambda m, pp: (lambda e: e.matmul(
                pb[4][:], shf_s[:, m, pp, :], ew3_s[:, 0, pp, 0:512],
                start=(pp == 0), stop=(pp == 3), skip_group_check=True)))(mt, p),
                waits=(wz if p == 0 else []))
            op("tensor", (lambda m, pp: (lambda e: e.matmul(
                pb[5][:], shf_s[:, m, pp, :], ew3_s[:, 0, pp, 512:1024],
                start=(pp == 0), stop=(pp == 3), skip_group_check=True)))(mt, p))
        st_eb[("pez", e_i, mt)] = mark("tensor")
        op("vector", (lambda m: (lambda e: e.scalar_tensor_tensor(
            partial_s[:, m, 0:512], pb[4][:], wex_s[:, m:m + 1], partial_s[:, m, 0:512],
            ALU.mult, ALU.add)))(mt), waits=[("pe", st_eb[("pez", e_i, mt)])], drain=False)
        op("vector", (lambda m: (lambda e: e.scalar_tensor_tensor(
            partial_s[:, m, 512:1024], pb[5][:], wex_s[:, m:m + 1], partial_s[:, m, 512:1024],
            ALU.mult, ALU.add)))(mt), drain=False)
        st_eb[("vstt", i)] = mark("vector")

    ew1_load(0)
    ew1_load(1)
    ew2_load(0)
    ew3_load(0)
    for i in range(NI + 1):
        ei, mt = divmod(i, MT)
        if i < NI:
            s1_h(i)
            if mt == MT - 1 and ei + 1 < E:
                ew2_load(ei + 1)
                if ei + 2 < E:
                    ew1_load(ei + 2)
        if i - 1 >= 0:
            s3_z(i - 1)
            ei1, mt1 = divmod(i - 1, MT)
            if mt1 == MT - 1 and ei1 + 1 < E:
                ew3_load(ei1 + 1)
        if i < NI:
            s2_vs(i)

    for mt in range(MT):
        op("tensor", (lambda m: (lambda e: e.matmul(
            pb[4][:], selt_s[0:E, m, :], eb3_s[:, 0:512],
            start=True, stop=True, skip_group_check=True)))(mt),
            waits=([("ve", v_selt), ("ve", st_eb[("vstt", NI - 1)])] if mt == 0
                   else [("ve", V("ve"))]))
        op("tensor", (lambda m: (lambda e: e.matmul(
            pb[5][:], selt_s[0:E, m, :], eb3_s[:, 512:1024],
            start=True, stop=True, skip_group_check=True)))(mt))
        pe_eb3 = mark("tensor")
        op("vector", (lambda m: (lambda e: e.scalar_tensor_tensor(
            partial_s[:, m, 0:512], pb[4][:], wex_s[:, m:m + 1], partial_s[:, m, 0:512],
            ALU.mult, ALU.add)))(mt), waits=[("pe", pe_eb3)], drain=False)
        op("vector", (lambda m: (lambda e: e.scalar_tensor_tensor(
            partial_s[:, m, 512:1024], pb[5][:], wex_s[:, m:m + 1], partial_s[:, m, 512:1024],
            ALU.mult, ALU.add)))(mt), drain=False)
        mark("vector")
    v_fin = mark("vector")
    dma("sync", domoe.rearrange("(m p) d -> p m d", p=128), partial_s[:],
        waits=[("ve", v_fin)], sem="oc")

    sch.emit("sync", lambda e: e.nop(),
             waits=[("oa", V("oa")), ("ob", V("ob")), ("oc", V("oc")), ("od", V("od"))])
    sch.emit("gpsimd", lambda e: e.nop(),
             waits=[("sw", V("sw")), ("wa", V("wa")), ("wb", V("wb")), ("wc", V("wc"))])

    with nc.Block() as block:
        def make_replay(name):
            def rep(eng):
                seen = {}
                for fn, waits, inc, amt, drain in sch.ops[name]:
                    for (s, v) in waits:
                        if seen.get(s, 0) < v:
                            eng.wait_ge(sems[s], v)
                            seen[s] = v
                    if (drain or RACE_DEBUG) and name in ("vector", "scalar", "gpsimd"):
                        eng.drain()
                    inst = fn(eng)
                    if inc is not None:
                        if inst is None:
                            inst = eng.nop()
                        inst.then_inc(sems[inc], amt)
            return rep

        for name in ENGINES:
            getattr(block, name)(make_replay(name))

    es.close()
    return nc


_NC_CACHE = None


def kernel(**inp):
    global _NC_CACHE
    f32 = np.float32
    import ml_dtypes
    bf = ml_dtypes.bfloat16
    C = np.ascontiguousarray

    x = np.asarray(inp["L0_attn_x"], dtype=f32)
    ib = np.asarray(inp["L0_ffn_index_b"]).astype(np.int32)
    it = np.asarray(inp["L0_ffn_index_t"]).astype(np.int32)

    shared = {
        "wq": C(inp["L0_attn_Wq"], dtype=f32),
        "wk": C(inp["L0_attn_Wk"], dtype=f32),
        "wv": C(inp["L0_attn_Wv"], dtype=f32),
        "wo": C(inp["L0_attn_Wo"], dtype=f32),
        "sw1t": C(np.asarray(inp["L0_ffn_sw1"]).T, dtype=f32),
        "sw2t": C(np.asarray(inp["L0_ffn_sw2"]).T, dtype=f32),
        "sw3t": C(np.asarray(inp["L0_ffn_sw3"]).T, dtype=f32),
        "ew1t": C(np.asarray(inp["L0_ffn_ew1"]).transpose(0, 2, 1), dtype=f32),
        "ew2t": C(np.asarray(inp["L0_ffn_ew2"]).transpose(0, 2, 1), dtype=f32),
        "ew3t": C(np.asarray(inp["L0_ffn_ew3"]).transpose(0, 2, 1), dtype=f32),
        "eb1": C(inp["L0_ffn_eb1"], dtype=f32),
        "eb2": C(inp["L0_ffn_eb2"], dtype=f32),
        "eb3": C(inp["L0_ffn_eb3"], dtype=f32),
        "gsw": C(np.concatenate([np.asarray(inp["L0_ffn_eg_w"], dtype=f32).T,
                                 np.asarray(inp["L0_ffn_sg_w"], dtype=f32).T], axis=1)),
        "gb9": C(np.broadcast_to(np.concatenate(
            [np.asarray(inp["L0_ffn_eg_b"], dtype=f32),
             np.asarray(inp["L0_ffn_sg_b"], dtype=f32)])[None, :], (128, E + 1))),
        "expb": C(np.broadcast_to(
            np.asarray(inp["L0_ffn_exp_bias"], dtype=f32)[None, :], (128, E))),
        "iota8": C(np.broadcast_to(np.arange(E, dtype=f32)[None, :], (128, E))),
        "iota16": C(np.broadcast_to(np.arange(16, dtype=f32)[None, :], (128, 16))),
        "selg": C(np.broadcast_to(
            np.asarray(inp["L0_ffn_sel_ln_g"], dtype=f32)[None, :], (128, H))),
        "selb": C(np.broadcast_to(
            np.asarray(inp["L0_ffn_sel_ln_b"], dtype=f32)[None, :], (128, H))),
        "slng": C(np.asarray(inp["L0_ffn_s_ln_g"], dtype=f32).reshape(4, 128).T),
        "slnb": C(np.asarray(inp["L0_ffn_s_ln_b"], dtype=f32).reshape(4, 128).T),
        "sb1": C(np.asarray(inp["L0_ffn_sb1"], dtype=f32)[None, :]),
        "sb2": C(np.asarray(inp["L0_ffn_sb2"], dtype=f32)[None, :]),
        "sb3": C(np.asarray(inp["L0_ffn_sb3"], dtype=f32)[None, :]),
        "bo": C(np.asarray(inp["L0_attn_bo"], dtype=f32)[None, :]),
        "idn": np.eye(128, dtype=bf),
        "idnf": np.eye(128, dtype=f32),
        "onesr": np.ones((1, 128), dtype=bf),
        "onesc": np.ones((128, 1), dtype=bf),
    }

    in_maps = []
    for c in range(NCORES):
        m = dict(shared)
        m["x"] = C(x[2 * c:2 * c + 2].reshape(BPC * N, D))
        sl = slice(c * MS, (c + 1) * MS)
        m["xmoe"] = C(x[ib[sl], it[sl]])
        m["idb"] = C(ib[sl].reshape(MT, 128).T)
        in_maps.append(m)

    if _NC_CACHE is None:
        _NC_CACHE = build_program()
    res = run_bass_kernel_spmd(_NC_CACHE, in_maps, core_ids=list(range(NCORES)))

    out = np.empty((B, N, D), dtype=f32)
    moe = np.empty((M, D), dtype=f32)
    for c in range(NCORES):
        out[2 * c:2 * c + 2] = np.asarray(res.results[c]["om"]).reshape(BPC, N, D)
        moe[c * MS:(c + 1) * MS] = np.asarray(res.results[c]["omoe"])
    out[ib, it] = moe
    return out


# version 35
# speedup vs baseline: 1.0271x; 1.0271x over previous
import numpy as np
from contextlib import ExitStack

import concourse.bass as bass
import concourse.mybir as mybir
from concourse.bass_utils import run_bass_kernel_spmd

F32 = mybir.dt.float32
BF16 = mybir.dt.bfloat16
I32 = mybir.dt.int32
U32 = mybir.dt.uint32
AF = mybir.ActivationFunctionType
ALU = mybir.AluOpType
AX = mybir.AxisListType

NCORES = 8
B, N, D = 16, 4096, 1024
HID = 512
HEADS, DH = 8, 64
SCALE = DH ** -0.5
H = 512
E = 8
M = 2048
BPC = 2
TPB = N // 128
MS = M // NCORES
MT = MS // 128
EPS = 1e-5

ENGINES = ("sync", "scalar", "vector", "tensor", "gpsimd")

RACE_DEBUG = False


class Sched:
    def __init__(self):
        self.ops = {e: [] for e in ENGINES}
        self.cnt = {}

    def val(self, sem):
        return self.cnt.get(sem, 0)

    def emit(self, eng, fn, waits=(), inc=None, amt=1, drain=True):
        ws = [(s, v) for (s, v) in waits if v > 0]
        self.ops[eng].append((fn, ws, inc, amt, drain))
        if inc is not None:
            self.cnt[inc] = self.cnt.get(inc, 0) + amt
        return self.cnt.get(inc, 0) if inc is not None else None


def build_program():
    nc = bass.Bass("TRN2", target_bir_lowering=False,
                   detect_race_conditions=RACE_DEBUG)

    dx = nc.dram_tensor("x", [BPC * N, D], F32, kind="ExternalInput").ap()
    dwq = nc.dram_tensor("wq", [D, HID], F32, kind="ExternalInput").ap()
    dwk = nc.dram_tensor("wk", [D, HID], F32, kind="ExternalInput").ap()
    dwv = nc.dram_tensor("wv", [D, HID], F32, kind="ExternalInput").ap()
    dwo = nc.dram_tensor("wo", [HID, D], F32, kind="ExternalInput").ap()
    dsw1t = nc.dram_tensor("sw1t", [D, H], F32, kind="ExternalInput").ap()
    dsw2t = nc.dram_tensor("sw2t", [D, H], F32, kind="ExternalInput").ap()
    dsw3t = nc.dram_tensor("sw3t", [H, D], F32, kind="ExternalInput").ap()
    dew1t = nc.dram_tensor("ew1t", [E, D, H], F32, kind="ExternalInput").ap()
    dew2t = nc.dram_tensor("ew2t", [E, D, H], F32, kind="ExternalInput").ap()
    dew3t = nc.dram_tensor("ew3t", [E, H, D], F32, kind="ExternalInput").ap()
    deb1 = nc.dram_tensor("eb1", [E, H], F32, kind="ExternalInput").ap()
    deb2 = nc.dram_tensor("eb2", [E, H], F32, kind="ExternalInput").ap()
    deb3 = nc.dram_tensor("eb3", [E, D], F32, kind="ExternalInput").ap()
    dgsw = nc.dram_tensor("gsw", [D, E + 1], F32, kind="ExternalInput").ap()
    dgb9 = nc.dram_tensor("gb9", [128, E + 1], F32, kind="ExternalInput").ap()
    dexpb = nc.dram_tensor("expb", [128, E], F32, kind="ExternalInput").ap()
    diota8 = nc.dram_tensor("iota8", [128, E], F32, kind="ExternalInput").ap()
    diota16 = nc.dram_tensor("iota16", [128, 16], F32, kind="ExternalInput").ap()
    dselg = nc.dram_tensor("selg", [128, H], F32, kind="ExternalInput").ap()
    dselb = nc.dram_tensor("selb", [128, H], F32, kind="ExternalInput").ap()
    dslng = nc.dram_tensor("slng", [128, 4], F32, kind="ExternalInput").ap()
    dslnb = nc.dram_tensor("slnb", [128, 4], F32, kind="ExternalInput").ap()
    dsb1 = nc.dram_tensor("sb1", [1, H], F32, kind="ExternalInput").ap()
    dsb2 = nc.dram_tensor("sb2", [1, H], F32, kind="ExternalInput").ap()
    dsb3 = nc.dram_tensor("sb3", [1, D], F32, kind="ExternalInput").ap()
    dbo = nc.dram_tensor("bo", [1, D], F32, kind="ExternalInput").ap()
    didb = nc.dram_tensor("idb", [128, MT], I32, kind="ExternalInput").ap()
    dxmoe = nc.dram_tensor("xmoe", [MS, D], F32, kind="ExternalInput").ap()
    didn = nc.dram_tensor("idn", [128, 128], BF16, kind="ExternalInput").ap()
    didnf = nc.dram_tensor("idnf", [128, 128], F32, kind="ExternalInput").ap()
    donesr = nc.dram_tensor("onesr", [1, 128], BF16, kind="ExternalInput").ap()
    donesc = nc.dram_tensor("onesc", [128, 1], BF16, kind="ExternalInput").ap()

    dom = nc.dram_tensor("om", [BPC * N, D], F32, kind="ExternalOutput").ap()
    domoe = nc.dram_tensor("omoe", [MS, D], F32, kind="ExternalOutput").ap()

    cc_in = nc.dram_tensor("cc_in", [BPC, 4, 128, 128], BF16).ap()
    cc_out = nc.dram_tensor("cc_out", [B, 4, 128, 128], BF16).ap()

    es = ExitStack()
    sbuf = lambda name, shape, dt: es.enter_context(nc.sbuf_tensor(name, shape, dt))
    psum = lambda name, shape, dt: es.enter_context(nc.psum_tensor(name, shape, dt))

    wq_s = sbuf("wq_s", [128, 8, HID], BF16)
    wk_s = sbuf("wk_s", [128, 8, HID], BF16)
    wv_s = sbuf("wv_s", [128, 8, HID], BF16)
    wo_s = sbuf("wo_s", [128, 4, D], BF16)
    sw1t_s = sbuf("sw1t_s", [128, 8, H], BF16)
    sw2t_s = sbuf("sw2t_s", [128, 8, H], BF16)
    sw3g_s = sbuf("sw3g_s", [128, 4, D], BF16)
    g_s = sbuf("g_s", [128, 4, D], BF16)
    gsw_s = sbuf("gsw_s", [128, 8, E + 1], BF16)
    eb3_s = sbuf("eb3_s", [E, D], BF16)
    ew1_s = sbuf("ew1_s", [128, 2, 8, H], BF16)
    ew2_s = sbuf("ew2_s", [128, 1, 8, H], BF16)
    ew3_s = sbuf("ew3_s", [128, 1, 4, D], BF16)
    ebrow_s = sbuf("ebrow_s", [1, 2, 2, H], BF16)
    sw3stage = sbuf("sw3stage", [128, 2, D], F32)[:]
    blkall_v = sbuf("blkall_s", [128, B * 4, 128], BF16)[:]
    idn_s = sbuf("idn_s", [128, 128], BF16)
    idnf_s = sbuf("idnf_s", [128, 128], F32)
    onesr_s = sbuf("onesr_s", [1, 128], BF16)
    onesc_s = sbuf("onesc_s", [128, 1], BF16)
    gb9_s = sbuf("gb9_s", [128, E + 1], F32)
    expb_s = sbuf("expb_s", [128, E], F32)
    iota8_s = sbuf("iota8_s", [128, E], F32)
    iota16_s = sbuf("iota16_s", [128, 16], F32)
    selg_s = sbuf("selg_s", [128, H], F32)
    selb_s = sbuf("selb_s", [128, H], F32)
    slng_s = sbuf("slng_s", [128, 4], F32)
    slnb_s = sbuf("slnb_s", [128, 4], F32)
    slnb_bf = sbuf("slnb_bf", [128, 4], BF16)
    sb1_s = sbuf("sb1_s", [1, H], BF16)
    sb2_s = sbuf("sb2_s", [1, H], BF16)
    bfin_s = sbuf("bfin_s", [1, D], BF16)
    bsh_bf = sbuf("bsh_bf", [1, D], BF16)
    bo_bf = sbuf("bo_bf", [1, D], BF16)
    xst_s = sbuf("xst_s", [128, 5, D], F32)
    sb3_f = xst_s[0:1, 3, :]
    bo_f = xst_s[0:1, 4, :]
    xmoe_v = xst_s[:, 2:4, :]
    xnt_s = sbuf("xnt_s", [128, 2, D], BF16)
    xnf_t = sbuf("xnf_t", [128, 2, 8, 128], BF16)
    mnri_s = sbuf("mnri_s", [128, 2, TPB], F32)
    kexp_s = sbuf("kexp_s", [128, 2, HID], BF16)
    vtok_s = sbuf("vtok_s", [128, 2, HID], BF16)
    qe32_s = sbuf("qe32_s", [128, 1, HID], F32)
    silu_s = sbuf("silu_s", [128, 1, H], F32)
    ksr_s = silu_s[0:1, 0, :].unsqueeze(0).squeeze(0).unsqueeze(0) if False else silu_s[0:1, 0, :]
    gate_s = sbuf("gate_s", [128, 2, H], F32)
    cacc_v = gate_s[:]
    brow_f = gate_s[0:1].rearrange("o a b -> o (a b)")
    qn_s = sbuf("qn_s", [128, 2, HID], BF16)
    sh_s = sbuf("sh_s", [128, 2, H], BF16)
    qf_s = sbuf("qf_s", [128, 2, 4, 128], BF16)
    shf_s = sbuf("shf_s", [128, 2, 4, 128], BF16)
    stats_s = sbuf("stats_s", [128, 12], F32)
    mv_s = sbuf("mv_s", [128, 2], F32)
    t0_s = sbuf("t0_s", [128, 1], F32)
    t1_s = sbuf("t1_s", [128, 1], F32)
    t2_s = sbuf("t2_s", [128, 1], F32)
    rinv_s = sbuf("rinv_s", [128, 1], F32)
    rs_s = sbuf("rs_s", [128, E], F32)
    blkt_s = sbuf("blkt_s", [128, 4, 128], BF16)
    blk_s = sbuf("blk_s", [128, 4, 128], BF16)

    ks4_s = sbuf("ks4_s", [128, 128], F32)
    invk_s = sbuf("invk_s", [128, 4], F32)
    xnmf_s = sbuf("xnmf_s", [128, 8, MS], BF16)
    idxf_s = sbuf("idxf_s", [128, MT], F32)
    idxi_s = sbuf("idxi_s", [128, MT], I32)
    scr_s = sbuf("scr_s", [128, MT, E + 1], F32)
    spr_s = sbuf("spr_s", [128, MT, E], F32)
    m8_s = sbuf("m8_s", [128, E], F32)
    ai_s = sbuf("ai_s", [128, E], U32)
    sel_s = sbuf("sel_s", [128, MT, E], BF16)
    self32_s = sbuf("self32_s", [128, MT, E], F32)
    selt_s = sbuf("selt_s", [E, MT, 128], BF16)
    wsh_s = sbuf("wsh_s", [128, MT], F32)
    wex_s = sbuf("wex_s", [128, MT], F32)
    maskb_s = sbuf("maskb_s", [128, MT, 16], F32)
    partial_s = sbuf("partial_s", [128, MT, D], F32)

    pb = [psum(f"pb{i}", [128, 512], F32) for i in range(6)]
    pt = [psum(f"pt{i}", [128, 1024], BF16) for i in range(2)]

    sems = {}
    for s in ["sx", "xa", "xb", "xc", "xd", "xe", "oa", "ob", "oc", "od", "oe", "of",
              "sw", "wa", "wb", "wc", "sk", "sm", "sl", "pe", "ve", "se", "ge", "cc"]:
        sems[s] = es.enter_context(nc.semaphore(s))

    sch = Sched()
    V = sch.val

    def dma(queue, out, in_, waits=(), sem=None):
        sem = sem or {"sync": "sx", "gpsimd": "sw", "scalar": "sx"}[queue]
        return sch.emit(queue, (lambda o, i: (lambda e: e.dma_start(out=o, in_=i)))(out, in_),
                        waits=waits, inc=sem, amt=16, drain=False)

    def op(eng, fn, waits=(), inc=False, drain=True):
        incmap = {"vector": "ve", "scalar": "se", "tensor": "pe", "gpsimd": "ge"}
        return sch.emit(eng, fn, waits=waits, inc=(incmap[eng] if inc else None),
                        drain=drain)

    def mark(eng):
        incmap = {"vector": "ve", "scalar": "se", "tensor": "pe", "gpsimd": "ge"}
        sem = incmap[eng]
        ops = sch.ops[eng]
        if ops and ops[-1][2] is None:
            fn, ws, _, _, dr = ops[-1]
            ops[-1] = (fn, ws, sem, 1, dr)
            sch.cnt[sem] = sch.cnt.get(sem, 0) + 1
            return sch.cnt[sem]
        return sch.emit(eng, lambda e: None, inc=sem)

    for out_s, in_d in [(idn_s[:], didn[:, :]), (idnf_s[:], didnf[:, :]),
                        (onesr_s[:], donesr[:, :]), (onesc_s[:], donesc[:, :]),
                        (gb9_s[:], dgb9[:, :]), (expb_s[:], dexpb[:, :]),
                        (iota8_s[:], diota8[:, :]), (iota16_s[:], diota16[:, :]),
                        (selg_s[:], dselg[:, :]),
                        (selb_s[:], dselb[:, :]), (slng_s[:], dslng[:, :]),
                        (slnb_s[:], dslnb[:, :]), (sb3_f, dsb3[:, :]),
                        (bo_f, dbo[:, :]),
                        (idxi_s[:], didb[:, :])]:
        dma("sync", out_s, in_d)
    n_const = V("sx")
    sw3_ld = dma("scalar", sw3stage,
                 dsw3t.rearrange("(k p) c -> p k c", p=128)[:, 0:2, :], sem="sk")

    def wload(out_s, in_d):
        dma("gpsimd", out_s, in_d.rearrange("(k p) c -> p k c", p=128))

    dma("gpsimd", wk_s[:], dwk.rearrange("(k p) c -> p k c", p=128), sem="wc")
    kv_w = dma("gpsimd", wv_s[:], dwv.rearrange("(k p) c -> p k c", p=128), sem="wc")
    wload(wq_s[:], dwq)
    wload(wo_s[:], dwo)
    wload(sw1t_s[:], dsw1t)
    wload(sw2t_s[:], dsw2t)
    wload(gsw_s[:], dgsw)
    dma("gpsimd", eb3_s[:], deb3[:, :])
    dma("gpsimd", sb1_s[:], dsb1[:, :])
    dma("gpsimd", sb2_s[:], dsb2[:, :])
    w_main = V("sw")

    op("vector", lambda e: e.memset(ks4_s[:], 0.0))
    for k in range(2):
        op("vector", (lambda kk: (lambda e: e.tensor_scalar_mul(
            sw3g_s[:, kk, :], sw3stage[:, kk, :], slng_s[:, kk:kk + 1])))(k),
            waits=[("sk", sw3_ld), ("sx", n_const)])
        op("vector", (lambda kk: (lambda e: e.tensor_copy(
            g_s[:, kk, :], sw3stage[:, kk, :])))(k))
    v_fold1 = mark("vector")
    sw3_ld2 = dma("scalar", sw3stage,
                  dsw3t.rearrange("(k p) c -> p k c", p=128)[:, 2:4, :],
                  waits=[("ve", v_fold1)], sem="sk")
    for k in range(2):
        op("vector", (lambda kk: (lambda e: e.tensor_scalar_mul(
            sw3g_s[:, kk + 2, :], sw3stage[:, kk, :], slng_s[:, kk + 2:kk + 3])))(k),
            waits=[("sk", sw3_ld2)])
        op("vector", (lambda kk: (lambda e: e.tensor_copy(
            g_s[:, kk + 2, :], sw3stage[:, kk, :])))(k))
    op("vector", lambda e: e.tensor_copy(slnb_bf[:], slnb_s[:]))
    op("vector", lambda e: e.tensor_copy(bo_bf[:], bo_f))
    v_sw3 = mark("vector")
    for k in range(4):
        for half in range(2):
            op("tensor", (lambda kk, hh: (lambda e: e.matmul(
                pb[hh][0:1, :], slnb_bf[:, kk:kk + 1],
                g_s[:, kk, hh * 512:(hh + 1) * 512],
                start=(kk == 0), stop=(kk == 3), skip_group_check=True)))(k, half),
                waits=([("ve", v_sw3)] if (k == 0 and half == 0) else []))
    pe_brow = mark("tensor")
    op("vector", lambda e: e.tensor_add(brow_f[0:1, 0:512], pb[0][0:1, :], sb3_f[0:1, 0:512]),
       waits=[("pe", pe_brow)])
    op("vector", lambda e: e.tensor_add(brow_f[0:1, 512:1024], pb[1][0:1, :], sb3_f[0:1, 512:1024]))
    op("vector", lambda e: e.tensor_copy(bsh_bf[0:1, :], brow_f[0:1, :]))
    op("vector", lambda e: e.tensor_add(brow_f[0:1, :], brow_f[0:1, :], bo_f))
    op("vector", lambda e: e.tensor_copy(bfin_s[0:1, :], brow_f[0:1, :]))
    v_bfin = mark("vector")

    def ln_stats(src_ap, fd, slot_mean, slot_rinv, waits, preload=None):
        if fd > 512:
            op("vector", lambda e: e.bn_stats(stats_s[:, 0:6], src_ap[:, 0:fd // 2]),
               waits=waits, drain=False)
            op("vector", lambda e: e.bn_stats(stats_s[:, 6:12], src_ap[:, fd // 2:fd]),
               drain=False)
            op("vector", lambda e: e.bn_aggr(mv_s[:], stats_s[:]))
        else:
            op("vector", (lambda s: (lambda e: e.bn_stats(stats_s[:, 0:6], s)))(src_ap),
               waits=waits, drain=False)
            op("vector", lambda e: e.bn_aggr(mv_s[:], stats_s[:, 0:6]))
        op("vector", lambda e: e.tensor_scalar_add(t0_s[:], mv_s[:, 1:2], EPS))
        vv = mark("vector")
        op("scalar", lambda e: e.sqrt(t1_s[:], t0_s[:]), waits=[("ve", vv)], drain=False)
        ss = mark("scalar")
        if False and preload is not None:
            op("scalar", (lambda pf: (lambda e: e.preload_act_table(pf)))(preload),
               drain=True)
        op("vector", (lambda sr: (lambda e: e.reciprocal(sr, t1_s[:])))(slot_rinv),
            waits=[("se", ss)], drain=False)
        if slot_mean is not None:
            op("vector", (lambda sm: (lambda e: e.tensor_copy(sm, mv_s[:, 0:1])))(slot_mean),
               drain=False)
        return mark("vector")

    tevac = {0: (None, 0), 1: (None, 0)}
    tsb = [0]

    def transpose_set(src_fn, nchunk, dst_ap, waits_pe, ev_eng="vector"):
        bank_id = tsb[0] % 2
        tsb[0] += 1
        bank = pt[bank_id]
        for c in range(nchunk):
            w = list(waits_pe) if c == 0 else []
            if c == 0 and tevac[bank_id][0] is not None:
                w.append(tevac[bank_id])
            op("tensor", (lambda cc: (lambda e: e.transpose(
                bank[:, cc * 128:(cc + 1) * 128], src_fn(cc), idn_s[:])))(c),
                waits=w)
        pe_n = mark("tensor")
        if ev_eng == "vector":
            op("vector", (lambda d, nch: (lambda e: e.tensor_copy(
                d, bank[:, 0:nch * 128].rearrange("p (c x) -> p c x", x=128))))(dst_ap, nchunk),
                waits=[("pe", pe_n)], drain=False)
            tevac[bank_id] = ("ve", mark("vector"))
            return V("ve"), V("se")
        else:
            op("scalar", (lambda d, nch: (lambda e: e.copy(
                d, bank[:, 0:nch * 128].rearrange("p (c x) -> p c x", x=128))))(dst_ap, nchunk),
                waits=[("pe", pe_n)], drain=False)
            tevac[bank_id] = ("se", mark("scalar"))
            return V("ve"), V("se")

    def apply_ln_tile(slot, t, waits=()):
        op("vector", (lambda sl, s3, tt: (lambda e: e.tensor_scalar(
            xnt_s[:, sl, :], xst_s[:, s3, :],
            mnri_s[:, 0, tt:tt + 1], mnri_s[:, 1, tt:tt + 1],
            ALU.subtract, ALU.mult)))(slot, t % 5, t), waits=waits)
        return mark("vector")

    out_dma = {}
    ve_x_done = {}
    s_kv_prev = 0
    pe_kst_prev = 0
    ks_ld_prev = 0
    cc_st_prev = 0
    xldA = {}
    xldB = {}
    evacsA = {}
    sB = {"s_sl": {}, "v_gate": {}, "v_out": {}, "v_sh": {}, "evF": {}, "evQ": {}}

    for b in range(BPC):
        def lastmod(tt):
            r = tt % 5
            return max(t2 for t2 in range(TPB) if t2 % 5 == r)

        def loadA(t, b=b):
            if t >= TPB:
                return
            s3 = t % 5
            r0 = b * N + t * 128
            w = [("ve", ve_x_done.get(("A", b, t - 5), 0))]
            if b == 0 and t in (3, 4):
                w.append(("ve", v_bfin))
            if b > 0:
                w.append(out_dma.get(("B", b - 1, lastmod(t)), ("oa", 0)))
            xsem = ["xa", "xb", "xc", "xd", "xe"][s3]
            xldA[(b, t)] = (xsem, dma("sync", xst_s[:, s3, :], dx[r0:r0 + 128, :],
                                      waits=w, sem=xsem))

        def lnA(t, b=b):
            s3 = t % 5
            loadA(t + 1)
            xsem, xv = xldA[(b, t)]
            ln_stats(xst_s[:, s3, :], D,
                     mnri_s[:, 0, t:t + 1], mnri_s[:, 1, t:t + 1], [(xsem, xv)])

        def applyTA(t, b=b):
            slot = t % 2
            v_xn = apply_ln_tile(slot, t)
            ve_x_done[("A", b, t)] = v_xn
            evacsA[(b, t)] = transpose_set(
                (lambda sl: (lambda c: xnt_s[:, sl, c * 128:(c + 1) * 128]))(slot), 8,
                xnf_t[:, slot, :, :], [("ve", v_xn)])

        def ctxA(t, b=b):
            slot = t % 2
            for p in range(4):
                op("tensor", (lambda pp, sl, tt: (lambda e: e.matmul(
                    pb[2 + pp][:, 0:128],
                    vtok_s[:, sl, pp * 128:(pp + 1) * 128],
                    kexp_s[:, sl, pp * 128:(pp + 1) * 128],
                    start=(tt == 0), stop=(tt == TPB - 1), skip_group_check=True)))(p, slot, t),
                    waits=([("se", sKVd[t])] if p == 0 else []))
                op("tensor", (lambda pp, sl, tt: (lambda e: e.matmul(
                    pb[2 + pp][0:1, 256:384], onesc_s[:],
                    kexp_s[:, sl, pp * 128:(pp + 1) * 128],
                    start=False, stop=(tt == TPB - 1), skip_group_check=True)))(p, slot, t))

        def backA(t, b=b):
            slot = t % 2
            v_evac, s_evac = evacsA[(b, t)]
            global_w = [("ve", v_evac), ("se", s_evac), ("se", sKV[0])]
            if b == 0 and t == 0:
                global_w.append(("wc", kv_w))
            for k in range(8):
                op("tensor", (lambda kk, sl: (lambda e: e.matmul(
                    pb[0][:], xnf_t[:, sl, kk, :], wk_s[:, kk, :],
                    start=(kk == 0), stop=(kk == 7), skip_group_check=True)))(k, slot),
                    waits=(global_w if k == 0 else []))
                op("tensor", (lambda kk, sl: (lambda e: e.matmul(
                    pb[1][:], xnf_t[:, sl, kk, :], wv_s[:, kk, :],
                    start=(kk == 0), stop=(kk == 7), skip_group_check=True)))(k, slot))
            pe_kv = mark("tensor")
            op("scalar", (lambda sl: (lambda e: e.activation(
                kexp_s[:, sl, :], pb[0][:], AF.Exp)))(slot), waits=[("pe", pe_kv)],
                drain=False)
            op("scalar", (lambda sl: (lambda e: e.copy(
                vtok_s[:, sl, :], pb[1][:])))(slot), drain=False)
            sKV[0] = mark("scalar")
            sKVd[t] = sKV[0]

        sKV = [s_kv_prev]
        sKVd = {}
        loadA(0)
        lnA(0)
        applyTA(0)
        for t in range(TPB):
            if t + 1 < TPB:
                lnA(t + 1)
            backA(t)
            if t - 1 >= 0:
                ctxA(t - 1)
            if t + 1 < TPB:
                applyTA(t + 1)
        ctxA(TPB - 1)
        s_kv_prev = sKV[0]
        pe_ctx = mark("tensor")

        if b == 0:
            op("vector", lambda e: e.memset(blkt_s[:], 0.0), waits=[("pe", pe_ctx)])
        for p in range(4):
            op("vector", (lambda pp: (lambda e: e.tensor_copy(
                blkt_s[0:64, pp, 0:64], pb[2 + pp][0:64, 0:64])))(p),
                waits=[("pe", pe_ctx), ("sk", ks_ld_prev)])
            op("vector", (lambda pp: (lambda e: e.tensor_copy(
                blkt_s[64:128, pp, 64:128], pb[2 + pp][64:128, 64:128])))(p))
            op("vector", (lambda pp: (lambda e: e.tensor_copy(
                ksr_s[0:1, pp * 128:(pp + 1) * 128], pb[2 + pp][0:1, 256:384])))(p))
        v_blkt = mark("vector")
        ks_ld = dma("sync", ks4_s[0:4, 0:128], ksr_s[0:1, :],
                    waits=[("ve", v_blkt), ("pe", pe_kst_prev)], sem="sk")
        ks_ld_prev = ks_ld
        pe_kst = op("tensor", lambda e: e.transpose(
            pb[0][:, 0:128], ks4_s[:], idnf_s[:]),
            waits=[("sk", ks_ld), ("se", V("se"))], inc=True)
        pe_kst_prev = pe_kst
        op("vector", lambda e: e.reciprocal(invk_s[:], pb[0][:, 0:4]), waits=[("pe", pe_kst)])
        op("vector", lambda e: e.tensor_scalar_mul(invk_s[:], invk_s[:], SCALE))
        v_invk = mark("vector")

        v_g = v_invk
        for p in range(4):
            for half in range(2):
                op("tensor", (lambda pp, hh: (lambda e: e.matmul(
                    pb[2 + hh][:], blkt_s[:, pp, :], wo_s[:, pp, hh * 512:(hh + 1) * 512],
                    start=True, stop=True, skip_group_check=True)))(p, half),
                    waits=(([("ve", v_g)] + ([("sw", w_main)] if b == 0 and p == 0 else []))
                           if half == 0 else []))
            pe_g = mark("tensor")
            op("vector", (lambda pp: (lambda e: e.tensor_scalar_mul(
                g_s[:, pp, 0:512], pb[2][:], invk_s[:, pp:pp + 1])))(p),
                waits=[("pe", pe_g)])
            op("vector", (lambda pp: (lambda e: e.tensor_scalar_mul(
                g_s[:, pp, 512:1024], pb[3][:], invk_s[:, pp:pp + 1])))(p))
            v_g = mark("vector")
            wbt = [("ve", v_g)]
            if tevac[0][0] is not None:
                wbt.append(tevac[0])
            pe_bt = op("tensor", (lambda pp: (lambda e: e.transpose(
                pt[0][:, 0:128], blkt_s[:, pp, :], idn_s[:])))(p),
                waits=wbt, inc=True)
            op("vector", (lambda pp: (lambda e: e.tensor_scalar_mul(
                blk_s[:, pp, :], pt[0][:, 0:128], invk_s[:, pp:pp + 1])))(p),
                waits=[("pe", pe_bt), ("sw", cc_st_prev)])
            tevac[0] = ("ve", mark("vector"))
        v_blk = tevac[0][1]
        cc_st = dma("gpsimd", cc_in[b:b + 1].rearrange("o q p c -> p (o q) c"), blk_s[:],
                    waits=[("ve", v_blk)])
        cc_st_prev = cc_st

        if b == BPC - 1:
            def do_cc(e):
                return e.collective_compute(
                    "AllGather", ALU.bypass,
                    replica_groups=[list(range(NCORES))],
                    ins=[cc_in.opt()],
                    outs=[cc_out.opt()])
            sch.emit("gpsimd", do_cc, waits=[("sw", V("sw"))], inc="cc", amt=1, drain=False)

        def loadB(t, b=b):
            if t >= TPB:
                return
            s3 = t % 5
            r0 = b * N + t * 128
            w = [out_dma.get(("B", b, t - 5), ("oa", 0)),
                 ("ve", ve_x_done.get(("A", b, lastmod(t)), 0))]
            if t < 5 and b > 0:
                w.append(out_dma.get(("B", b - 1, lastmod(t)), ("oa", 0)))
            xsem = ["xa", "xb", "xc", "xd", "xe"][s3]
            xldB[(b, t)] = (xsem, dma("sync", xst_s[:, s3, :], dx[r0:r0 + 128, :],
                                      waits=w, sem=xsem))

        def frontB(t, b=b):
            slot = t % 2
            loadB(t + 1)
            xsem, xv = xldB[(b, t)]
            v_xn = apply_ln_tile(slot, t, waits=[(xsem, xv)])
            ve_x_done[("B", b, t)] = v_xn
            sB["evF"][t] = transpose_set(
                (lambda sl: (lambda c: xnt_s[:, sl, c * 128:(c + 1) * 128]))(slot), 8,
                xnf_t[:, slot, :, :], [("ve", v_xn)])

        def backEarlyPE(t, b=b):
            slot = t % 2
            v_evac, s_evac = sB["evF"][t]
            wq_w = [("ve", v_evac), ("se", s_evac),
                    ("se", sB["s_sl"].get(t - 1, 0)), ("ve", sB["v_gate"].get(t - 1, 0))]
            if b == 0 and t == 0:
                wq_w.append(("sw", w_main))
            for k in range(8):
                op("tensor", (lambda kk, sl: (lambda e: e.matmul(
                    pb[0][:], xnf_t[:, sl, kk, :], wq_s[:, kk, :],
                    start=(kk == 0), stop=(kk == 7), skip_group_check=True)))(k, slot),
                    waits=(wq_w if k == 0 else []))
                op("tensor", (lambda kk, sl: (lambda e: e.matmul(
                    pb[1][:], xnf_t[:, sl, kk, :], sw1t_s[:, kk, :],
                    start=(kk == 0), stop=False, skip_group_check=True)))(k, slot))
                op("tensor", (lambda kk, sl: (lambda e: e.matmul(
                    pb[2][:], xnf_t[:, sl, kk, :], sw2t_s[:, kk, :],
                    start=(kk == 0), stop=False, skip_group_check=True)))(k, slot))
            op("tensor", lambda e: e.matmul(pb[1][:], onesr_s[:], sb1_s[:],
                                            start=False, stop=True, skip_group_check=True))
            op("tensor", lambda e: e.matmul(pb[2][:], onesr_s[:], sb2_s[:],
                                            start=False, stop=True, skip_group_check=True))
            sB["pe_qx"][t] = mark("tensor")

        def backEarlyVS(t, b=b):
            slot = t % 2
            pe_qx = sB["pe_qx"][t]
            op("scalar", lambda e: e.activation(
                qe32_s[:, 0, :], pb[0][:], AF.Exp), waits=[("pe", pe_qx)], drain=False)
            s_q = mark("scalar")
            op("scalar", lambda e: e.activation(
                silu_s[:, 0, :], pb[1][:], AF.Tanh, scale=0.5), drain=False)
            sB["s_sl"][t] = mark("scalar")

            op("vector", lambda e: e.reduce_sum(
                rs_s[:].unsqueeze(2), qe32_s[:, 0, :].rearrange("p (h d) -> p h d", d=DH),
                axis=AX.X), waits=[("se", s_q)], drain=False)
            op("vector", lambda e: e.reciprocal(rs_s[:], rs_s[:]))
            op("vector", (lambda sl: (lambda e: e.tensor_tensor(
                qn_s[:, sl, :].rearrange("p (h d) -> p h d", d=DH),
                qe32_s[:, 0, :].rearrange("p (h d) -> p h d", d=DH),
                rs_s[:].unsqueeze(2).broadcast_to([128, E, DH]), ALU.mult)))(slot))
            op("vector", (lambda sl: (lambda e: e.scalar_tensor_tensor(
                gate_s[:, sl, :], silu_s[:, 0, :], 1.0, pb[1][:],
                ALU.add, ALU.mult)))(slot),
                waits=[("se", sB["s_sl"][t])], drain=False)
            op("vector", (lambda sl: (lambda e: e.scalar_tensor_tensor(
                gate_s[:, sl, :], gate_s[:, sl, :], 0.5, pb[2][:],
                ALU.mult, ALU.mult)))(slot))
            sB["v_gate"][t] = mark("vector")
            ln_stats(gate_s[:, slot, :], H, None, rinv_s[:], [], preload=AF.Exp)
            op("vector", (lambda sl: (lambda e: e.tensor_scalar(
                sh_s[:, sl, :], gate_s[:, sl, :], mv_s[:, 0:1], rinv_s[:, 0:1],
                ALU.subtract, ALU.mult)))(slot))
            sB["v_sh"][t] = mark("vector")

        def backLatePE(t, b=b):
            slot = t % 2
            transpose_set((lambda sl: (lambda c: qn_s[:, sl, c * 128:(c + 1) * 128]))(slot), 4,
                          qf_s[:, slot, :, :], [("ve", sB["v_sh"][t])], ev_eng="scalar")
            v_evac, s_evac = transpose_set(
                (lambda sl: (lambda c: sh_s[:, sl, c * 128:(c + 1) * 128]))(slot), 4,
                shf_s[:, slot, :, :], [], ev_eng="scalar")

            wcs = [("ve", v_evac), ("se", s_evac), ("ve", sB["v_out"].get(t - 1, 0))]
            first = True
            for p in range(4):
                op("tensor", (lambda pp, sl: (lambda e: e.matmul(
                    pb[3][:], qf_s[:, sl, pp, :], g_s[:, pp, 0:512],
                    start=(pp == 0), stop=False, skip_group_check=True)))(p, slot),
                    waits=(wcs if first else []))
                first = False
                op("tensor", (lambda pp, sl: (lambda e: e.matmul(
                    pb[4][:], qf_s[:, sl, pp, :], g_s[:, pp, 512:1024],
                    start=(pp == 0), stop=False, skip_group_check=True)))(p, slot))
            for p in range(4):
                op("tensor", (lambda pp, sl: (lambda e: e.matmul(
                    pb[3][:], shf_s[:, sl, pp, :], sw3g_s[:, pp, 0:512],
                    start=False, stop=False, skip_group_check=True)))(p, slot))
                op("tensor", (lambda pp, sl: (lambda e: e.matmul(
                    pb[4][:], shf_s[:, sl, pp, :], sw3g_s[:, pp, 512:1024],
                    start=False, stop=False, skip_group_check=True)))(p, slot))
            op("tensor", lambda e: e.matmul(pb[3][:], onesr_s[:], bfin_s[0:1, 0:512],
                                            start=False, stop=True, skip_group_check=True),
               waits=([("ve", v_bfin)] if (b == 0 and t == 0) else []))
            op("tensor", lambda e: e.matmul(pb[4][:], onesr_s[:], bfin_s[0:1, 512:1024],
                                            start=False, stop=True, skip_group_check=True))
            sB["pe_cs"][t] = mark("tensor")

        def backLateFin(t, b=b):
            pe_cs = sB["pe_cs"][t]
            r0 = b * N + t * 128
            s3 = t % 5
            op("vector", (lambda s3: (lambda e: e.tensor_add(
                xst_s[:, s3, 0:512], xst_s[:, s3, 0:512], pb[3][:])))(s3),
                waits=[("pe", pe_cs)], drain=False)
            op("vector", (lambda s3: (lambda e: e.tensor_add(
                xst_s[:, s3, 512:1024], xst_s[:, s3, 512:1024], pb[4][:])))(s3),
                drain=False)
            v_out = mark("vector")
            sB["v_out"][t] = v_out
            osem = ["oa", "ob", "od", "oe", "of"][s3]
            so = sch.emit("sync", (lambda s3, rr: (lambda e: e.dma_start(
                out=dom[rr:rr + 128, :], in_=xst_s[:, s3, :])))(s3, r0),
                waits=[("ve", v_out)], inc=osem, amt=16)
            out_dma[("B", b, t)] = (osem, so)

        sB = {"s_sl": {}, "v_gate": {}, "v_out": {}, "v_sh": {}, "evF": {},
              "pe_cs": {}, "pe_qx": {}}
        loadB(0)
        for j in range(TPB + 3):
            if 0 <= j - 1 < TPB:
                backEarlyPE(j - 1)
            if j - 3 >= 0:
                backLatePE(j - 3)
            if j < TPB:
                frontB(j)
            if 0 <= j - 1 < TPB:
                backEarlyVS(j - 1)
            if j - 3 >= 0:
                backLateFin(j - 3)

    xm_ld = dma("sync", xmoe_v, dxmoe.rearrange("(m p) d -> p m d", p=128),
                waits=[out_dma[("B", BPC - 1, 27)],
                       out_dma[("B", BPC - 1, 28)]], sem="sm")
    blkall_ld = dma("sync", blkall_v, cc_out.rearrange("b q p c -> p (b q) c"),
                    waits=[("cc", 1), ("ve", v_bfin)], sem="sl")

    moe_w = [("sm", xm_ld), ("ve", V("ve")), ("se", V("se")), ("pe", V("pe"))]
    for mt in range(MT):
        ln_stats(xmoe_v[:, mt, :], D, t0_s[:], rinv_s[:], moe_w if mt == 0 else [])
        op("vector", (lambda m: (lambda e: e.tensor_scalar(
            xnt_s[:, m, :], xmoe_v[:, m, :], t0_s[:, 0:1], rinv_s[:, 0:1],
            ALU.subtract, ALU.mult)))(mt))
        v_xnm = mark("vector")
        transpose_set((lambda m: (lambda c: xnt_s[:, m, c * 128:(c + 1) * 128]))(mt), 8,
                      xnmf_s[:, :, mt * 128:(mt + 1) * 128], [("ve", v_xnm)])
    v_evac, s_evac = V("ve"), V("se")

    op("vector", lambda e: e.tensor_copy(idxf_s[:], idxi_s[:]), waits=[("sx", n_const)])
    v_idx = mark("vector")

    v_scr_prev = 0
    s_q_prev = 0
    for mt in range(MT):
        for k in range(8):
            op("tensor", (lambda kk, m: (lambda e: e.matmul(
                pb[0][:, 0:E + 1], xnmf_s[:, kk, m * 128:(m + 1) * 128], gsw_s[:, kk, :],
                start=(kk == 0), stop=(kk == 7), skip_group_check=True)))(k, mt),
                waits=(([("ve", v_evac), ("se", s_evac)] if mt == 0
                        else [("ve", v_scr_prev)]) if k == 0 else []))
        pe_sc = mark("tensor")
        op("vector", (lambda m: (lambda e: e.tensor_add(
            scr_s[:, m, :], pb[0][:, 0:E + 1], gb9_s[:])))(mt), waits=[("pe", pe_sc)])
        v_scr_prev = mark("vector")
        op("scalar", (lambda m: (lambda e: e.activation(
            scr_s[:, m, :], scr_s[:, m, :], AF.Tanh, scale=0.5)))(mt),
            waits=[("ve", v_scr_prev)])
        s_scr = mark("scalar")
        op("vector", (lambda m: (lambda e: e.tensor_scalar(
            scr_s[:, m, :], scr_s[:, m, :], 0.5, 0.5, ALU.mult, ALU.add)))(mt),
            waits=[("se", s_scr)])
        s_scr_v = mark("vector")
        op("vector", (lambda m: (lambda e: e.tensor_add(
            spr_s[:, m, :], scr_s[:, m, 0:E], expb_s[:])))(mt))
        op("vector", (lambda m: (lambda e: e.max(m8_s[:], spr_s[:, m, :])))(mt))
        op("vector", (lambda m: (lambda e: e.max_index(ai_s[:], m8_s[:], spr_s[:, m, :])))(mt))
        op("vector", lambda e: e.tensor_copy(t0_s[:], ai_s[:, 0:1]))
        op("vector", (lambda m: (lambda e: e.tensor_single_scalar(
            self32_s[:, m, :], iota8_s[:], t0_s[:, 0:1], ALU.is_equal)))(mt))
        op("vector", (lambda m: (lambda e: e.tensor_copy(sel_s[:, m, :], self32_s[:, m, :])))(mt))
        op("vector", (lambda m: (lambda e: e.tensor_tensor(
            spr_s[:, m, :], scr_s[:, m, 0:E], self32_s[:, m, :], ALU.mult)))(mt))
        op("vector", (lambda m: (lambda e: e.reduce_sum(
            t1_s[:], spr_s[:, m, :], axis=AX.X)))(mt))
        op("vector", (lambda m: (lambda e: e.tensor_tensor(
            t0_s[:], scr_s[:, m, E:E + 1], t1_s[:], ALU.max)))(mt))
        op("vector", (lambda m: (lambda e: e.tensor_tensor(
            t2_s[:], scr_s[:, m, E:E + 1], t0_s[:], ALU.subtract)))(mt))
        op("vector", lambda e: e.tensor_sub(t1_s[:], t1_s[:], t0_s[:]))
        v_pre = mark("vector")
        op("scalar", lambda e: e.activation(t2_s[:], t2_s[:], AF.Exp), waits=[("ve", v_pre)])
        op("scalar", lambda e: e.activation(t1_s[:], t1_s[:], AF.Exp))
        s_ex = mark("scalar")
        op("vector", lambda e: e.tensor_add(t0_s[:], t1_s[:], t2_s[:]), waits=[("se", s_ex)])
        op("vector", lambda e: e.reciprocal(t0_s[:], t0_s[:]))
        op("vector", (lambda m: (lambda e: e.tensor_mul(wsh_s[:, m:m + 1], t2_s[:], t0_s[:])))(mt))
        op("vector", (lambda m: (lambda e: e.tensor_mul(wex_s[:, m:m + 1], t1_s[:], t0_s[:])))(mt))
        v_scr_prev = mark("vector")

    v_qn_prev = 0
    for mt in range(MT):
        for k in range(8):
            op("tensor", (lambda kk, m: (lambda e: e.matmul(
                pb[0][:], xnmf_s[:, kk, m * 128:(m + 1) * 128], wq_s[:, kk, :],
                start=(kk == 0), stop=(kk == 7), skip_group_check=True)))(k, mt),
                waits=(([("ve", v_scr_prev)] if mt == 0
                        else [("se", s_q_prev), ("ve", v_qn_prev)])
                       if k == 0 else []))
        pe_q = mark("tensor")
        op("scalar", lambda e: e.activation(
            qe32_s[:, 0, :], pb[0][:], AF.Exp), waits=[("pe", pe_q)])
        s_q_prev = mark("scalar")
        op("vector", lambda e: e.reduce_sum(
            rs_s[:].unsqueeze(2), qe32_s[:, 0, :].rearrange("p (h d) -> p h d", d=DH),
            axis=AX.X), waits=[("se", s_q_prev)])
        op("vector", lambda e: e.reciprocal(rs_s[:], rs_s[:]))
        op("vector", (lambda m: (lambda e: e.tensor_tensor(
            qn_s[:, m, :].rearrange("p (h d) -> p h d", d=DH),
            qe32_s[:, 0, :].rearrange("p (h d) -> p h d", d=DH),
            rs_s[:].unsqueeze(2).broadcast_to([128, E, DH]), ALU.mult)))(mt))
        v_qn = mark("vector")
        v_qn_prev = v_qn
        transpose_set((lambda m: (lambda c: qn_s[:, m, c * 128:(c + 1) * 128]))(mt), 4,
                      qf_s[:, mt, :, :], [("ve", v_qn)])
    v_evac, s_evac = V("ve"), V("se")

    for mt in range(MT):
        op("vector", (lambda m: (lambda e: e.memset(cacc_v[:, m, :], 0.0)))(mt))
        op("vector", (lambda m: (lambda e: e.tensor_single_scalar(
            maskb_s[:, m, :], iota16_s[:], idxf_s[:, m:m + 1], ALU.is_equal)))(mt),
            waits=[("ve", v_idx)] if mt == 0 else [])
    v_cz = mark("vector")
    prev_cb = {0: 0, 1: 0}
    last_cb = {}
    for bb in range(B):
        for mt in range(MT):
            i2 = (bb * MT + mt) % 2
            wcb = ([("ve", v_cz), ("se", s_evac), ("sl", blkall_ld)]
                   if (bb == 0 and mt == 0) else [("ve", prev_cb[i2])])
            for p in range(4):
                op("tensor", (lambda m, bbb, pp, bk: (lambda e: e.matmul(
                    pb[bk][:, pp * 128:(pp + 1) * 128], qf_s[:, m, pp, :],
                    blkall_v[:, bbb * 4 + pp, :],
                    start=(pp == 0), stop=(pp == 3), skip_group_check=True)))(mt, bb, p, i2),
                    waits=(wcb if p == 0 else []))
            pe_cb = mark("tensor")
            op("vector", (lambda m, bbb, bk: (lambda e: e.scalar_tensor_tensor(
                cacc_v[:, m, :], pb[bk][:], maskb_s[:, m, bbb:bbb + 1], cacc_v[:, m, :],
                ALU.mult, ALU.add)))(mt, bb, i2), waits=[("pe", pe_cb)])
            prev_cb[i2] = mark("vector")
            last_cb[mt] = prev_cb[i2]

    for mt in range(MT):
        op("vector", (lambda m: (lambda e: e.tensor_copy(
            qn_s[:, m, :], cacc_v[:, m, :])))(mt), waits=[("ve", last_cb[MT - 1])])
        v_cb = mark("vector")
        v_evac, s_evac = transpose_set(
            (lambda m: (lambda c: qn_s[:, m, c * 128:(c + 1) * 128]))(mt), 4,
            qf_s[:, mt, :, :], [("ve", v_cb)])
        for p in range(4):
            op("tensor", (lambda m, pp: (lambda e: e.matmul(
                pb[3][:], qf_s[:, m, pp, :], wo_s[:, pp, 0:512],
                start=(pp == 0), stop=False, skip_group_check=True)))(mt, p),
                waits=([("ve", v_evac), ("se", s_evac)] if p == 0 else []))
            op("tensor", (lambda m, pp: (lambda e: e.matmul(
                pb[4][:], qf_s[:, m, pp, :], wo_s[:, pp, 512:1024],
                start=(pp == 0), stop=False, skip_group_check=True)))(mt, p))
        op("tensor", lambda e: e.matmul(pb[3][:], onesr_s[:], bo_bf[0:1, 0:512],
                                        start=False, stop=True, skip_group_check=True))
        op("tensor", lambda e: e.matmul(pb[4][:], onesr_s[:], bo_bf[0:1, 512:1024],
                                        start=False, stop=True, skip_group_check=True))
        mark("tensor")

        for k in range(8):
            op("tensor", (lambda kk, m: (lambda e: e.matmul(
                pb[1][:], xnmf_s[:, kk, m * 128:(m + 1) * 128], sw1t_s[:, kk, :],
                start=(kk == 0), stop=False, skip_group_check=True)))(k, mt),
                waits=([("ve", V("ve"))] if (k == 0 and mt == 1) else []))
            op("tensor", (lambda kk, m: (lambda e: e.matmul(
                pb[2][:], xnmf_s[:, kk, m * 128:(m + 1) * 128], sw2t_s[:, kk, :],
                start=(kk == 0), stop=False, skip_group_check=True)))(k, mt))
        op("tensor", lambda e: e.matmul(pb[1][:], onesr_s[:], sb1_s[:],
                                        start=False, stop=True, skip_group_check=True))
        op("tensor", lambda e: e.matmul(pb[2][:], onesr_s[:], sb2_s[:],
                                        start=False, stop=True, skip_group_check=True))
        pe_x12 = mark("tensor")
        op("scalar", lambda e: e.activation(
            silu_s[:, 0, :], pb[1][:], AF.Tanh, scale=0.5), waits=[("pe", pe_x12)])
        s_sl = mark("scalar")
        op("vector", lambda e: e.tensor_scalar(
            silu_s[:, 0, :], silu_s[:, 0, :], 0.5, 0.5, ALU.mult, ALU.add),
            waits=[("se", s_sl)])
        op("vector", (lambda m: (lambda e: e.tensor_tensor(
            gate_s[:, m, :], silu_s[:, 0, :], pb[1][:], ALU.mult)))(mt))
        op("vector", (lambda m: (lambda e: e.tensor_tensor(
            gate_s[:, m, :], gate_s[:, m, :], pb[2][:], ALU.mult)))(mt))
        ln_stats(gate_s[:, mt, :], H, t0_s[:], rinv_s[:], [])
        op("vector", (lambda m: (lambda e: e.tensor_scalar(
            sh_s[:, m, :], gate_s[:, m, :], t0_s[:, 0:1], rinv_s[:, 0:1],
            ALU.subtract, ALU.mult)))(mt))
        v_sh = mark("vector")
        v_evac2, s_evac2 = transpose_set(
            (lambda m: (lambda c: sh_s[:, m, c * 128:(c + 1) * 128]))(mt), 4,
            shf_s[:, mt, :, :], [("ve", v_sh)])
        for p in range(4):
            op("tensor", (lambda m, pp: (lambda e: e.matmul(
                pb[2][:], shf_s[:, m, pp, :], sw3g_s[:, pp, 0:512],
                start=(pp == 0), stop=False, skip_group_check=True)))(mt, p),
                waits=([("ve", v_evac2), ("se", s_evac2)] if p == 0 else []))
            op("tensor", (lambda m, pp: (lambda e: e.matmul(
                pb[5][:], shf_s[:, m, pp, :], sw3g_s[:, pp, 512:1024],
                start=(pp == 0), stop=False, skip_group_check=True)))(mt, p))
        op("tensor", lambda e: e.matmul(pb[2][:], onesr_s[:], bsh_bf[0:1, 0:512],
                                        start=False, stop=True, skip_group_check=True))
        op("tensor", lambda e: e.matmul(pb[5][:], onesr_s[:], bsh_bf[0:1, 512:1024],
                                        start=False, stop=True, skip_group_check=True))
        pe_sh = mark("tensor")

        op("vector", (lambda m: (lambda e: e.scalar_tensor_tensor(
            partial_s[:, m, 0:512], pb[2][:], wsh_s[:, m:m + 1], xmoe_v[:, m, 0:512],
            ALU.mult, ALU.add)))(mt), waits=[("pe", pe_sh)])
        op("vector", (lambda m: (lambda e: e.scalar_tensor_tensor(
            partial_s[:, m, 512:1024], pb[5][:], wsh_s[:, m:m + 1], xmoe_v[:, m, 512:1024],
            ALU.mult, ALU.add)))(mt))
        op("vector", (lambda m: (lambda e: e.tensor_add(
            partial_s[:, m, 0:512], partial_s[:, m, 0:512], pb[3][:])))(mt))
        op("vector", (lambda m: (lambda e: e.tensor_add(
            partial_s[:, m, 512:1024], partial_s[:, m, 512:1024], pb[4][:])))(mt))
    v_part = mark("vector")

    for mt in range(MT):
        wst = [("ve", v_part)]
        if tevac[0][0] is not None:
            wst.append(tevac[0])
        pe_st = op("tensor", (lambda m: (lambda e: e.transpose(
            pt[0][0:E, 0:128], sel_s[:, m, :], idn_s[:])))(mt),
            waits=wst, inc=True)
        op("vector", (lambda m: (lambda e: e.tensor_copy(
            selt_s[0:E, m, :], pt[0][0:E, 0:128])))(mt), waits=[("pe", pe_st)])
        tevac[0] = ("ve", mark("vector"))
    v_selt = tevac[0][1]

    hpair = {0: (pb[0], pb[1]), 1: (pb[2], pb[3])}
    NI = E * MT
    st_eb = {}

    def ew1_load(e_i):
        buf = e_i % 2
        wsem = "wa" if buf == 0 else "wb"
        w = ([] if e_i < 2 else [("pe", st_eb.get(("peh", e_i - 2), 0))])
        dma("gpsimd", ew1_s[:, buf, :, :],
            dew1t[e_i:e_i + 1].rearrange("o (k p) c -> p (o k) c", p=128), waits=w, sem=wsem)
        dma("gpsimd", ebrow_s[0:1, buf, 0, :], deb1[e_i:e_i + 1, :], sem=wsem)
        st_eb[("d1", e_i)] = (wsem, dma(
            "gpsimd", ebrow_s[0:1, buf, 1, :], deb2[e_i:e_i + 1, :], sem=wsem))

    def ew2_load(e_i):
        wsem = "wa" if (e_i % 2) == 0 else "wb"
        w = ([] if e_i == 0 else [("pe", st_eb[("peh_i", e_i * MT - 1)])])
        st_eb[("d2", e_i)] = (wsem, dma(
            "gpsimd", ew2_s[:, 0, :, :],
            dew2t[e_i:e_i + 1].rearrange("o (k p) c -> p (o k) c", p=128),
            waits=w, sem=wsem))

    def ew3_load(e_i):
        w = ([] if e_i == 0 else [("pe", st_eb[("pez", e_i - 1, MT - 1)])])
        st_eb[("d3", e_i)] = ("wc", dma(
            "gpsimd", ew3_s[:, 0, :, :],
            dew3t[e_i:e_i + 1].rearrange("o (k p) c -> p (o k) c", p=128),
            waits=w, sem="wc"))

    def s1_h(i):
        e_i, mt = divmod(i, MT)
        buf = e_i % 2
        ph1, ph2 = hpair[i % 2]
        wsem = "wa" if buf == 0 else "wb"
        wh = [st_eb[("d2", e_i)],
              ("ve", st_eb.get(("vgate", i - 2), 0)),
              ("se", st_eb.get(("setanh", i - 2), 0))]
        if i == 0:
            wh += [("ve", V("ve")), ("se", V("se"))]
        for k in range(8):
            op("tensor", (lambda kk, m, bf, p1: (lambda e: e.matmul(
                p1[:], xnmf_s[:, kk, m * 128:(m + 1) * 128], ew1_s[:, bf, kk, :],
                start=(kk == 0), stop=False, skip_group_check=True)))(k, mt, buf, ph1),
                waits=(wh if k == 0 else []))
            op("tensor", (lambda kk, m, p2: (lambda e: e.matmul(
                p2[:], xnmf_s[:, kk, m * 128:(m + 1) * 128], ew2_s[:, 0, kk, :],
                start=(kk == 0), stop=False, skip_group_check=True)))(k, mt, ph2))
        op("tensor", (lambda bf, p1: (lambda e: e.matmul(
            p1[:], onesr_s[:], ebrow_s[0:1, bf, 0, :],
            start=False, stop=True, skip_group_check=True)))(buf, ph1))
        op("tensor", (lambda bf, p2: (lambda e: e.matmul(
            p2[:], onesr_s[:], ebrow_s[0:1, bf, 1, :],
            start=False, stop=True, skip_group_check=True)))(buf, ph2))
        st_eb[("peh_i", i)] = mark("tensor")
        e_done = (mt == MT - 1)
        if e_done:
            st_eb[("peh", e_i)] = st_eb[("peh_i", i)]

    def s2_vs(i):
        e_i, mt = divmod(i, MT)
        ph1, ph2 = hpair[i % 2]
        op("scalar", (lambda p1: (lambda e: e.activation(
            silu_s[:, 0, :], p1[:], AF.Tanh, scale=0.5)))(ph1),
            waits=[("pe", st_eb[("peh_i", i)])], drain=False)
        st_eb[("setanh", i)] = mark("scalar")
        op("vector", (lambda m, p1: (lambda e: e.scalar_tensor_tensor(
            gate_s[:, m, :], silu_s[:, 0, :], 1.0, p1[:], ALU.add, ALU.mult)))(mt, ph1),
            waits=[("se", st_eb[("setanh", i)])], drain=False)
        op("vector", (lambda m, p2: (lambda e: e.scalar_tensor_tensor(
            gate_s[:, m, :], gate_s[:, m, :], 0.5, p2[:], ALU.mult, ALU.mult)))(mt, ph2))
        st_eb[("vgate", i)] = mark("vector")
        ln_stats(gate_s[:, mt, :], H, None, rinv_s[:], [])
        op("vector", (lambda m, ee: (lambda e: e.tensor_mul(
            t2_s[:], rinv_s[:], self32_s[:, m, ee:ee + 1])))(mt, e_i))
        op("vector", (lambda m: (lambda e: e.tensor_scalar(
            gate_s[:, m, :], gate_s[:, m, :], mv_s[:, 0:1], t2_s[:, 0:1],
            ALU.subtract, ALU.mult)))(mt))
        op("vector", (lambda m: (lambda e: e.tensor_tensor(
            gate_s[:, m, :], gate_s[:, m, :], selg_s[:], ALU.mult)))(mt))
        op("vector", (lambda m, ee: (lambda e: e.scalar_tensor_tensor(
            sh_s[:, m, :], selb_s[:], self32_s[:, m, ee:ee + 1], gate_s[:, m, :],
            ALU.mult, ALU.add)))(mt, e_i))
        st_eb[("vz", i)] = mark("vector")

    def s3_z(i):
        e_i, mt = divmod(i, MT)
        transpose_set((lambda m: (lambda c: sh_s[:, m, c * 128:(c + 1) * 128]))(mt), 4,
                      shf_s[:, mt, :, :], [("ve", st_eb[("vz", i)])], ev_eng="scalar")
        vv, ss = V("ve"), V("se")
        wz = [("se", ss), ("ve", st_eb.get(("vstt", i - 1), 0)),
              st_eb[("d3", e_i)]]
        for p in range(4):
            op("tensor", (lambda m, pp: (lambda e: e.matmul(
                pb[4][:], shf_s[:, m, pp, :], ew3_s[:, 0, pp, 0:512],
                start=(pp == 0), stop=(pp == 3), skip_group_check=True)))(mt, p),
                waits=(wz if p == 0 else []))
            op("tensor", (lambda m, pp: (lambda e: e.matmul(
                pb[5][:], shf_s[:, m, pp, :], ew3_s[:, 0, pp, 512:1024],
                start=(pp == 0), stop=(pp == 3), skip_group_check=True)))(mt, p))
        st_eb[("pez", e_i, mt)] = mark("tensor")
        op("vector", (lambda m: (lambda e: e.scalar_tensor_tensor(
            partial_s[:, m, 0:512], pb[4][:], wex_s[:, m:m + 1], partial_s[:, m, 0:512],
            ALU.mult, ALU.add)))(mt), waits=[("pe", st_eb[("pez", e_i, mt)])], drain=False)
        op("vector", (lambda m: (lambda e: e.scalar_tensor_tensor(
            partial_s[:, m, 512:1024], pb[5][:], wex_s[:, m:m + 1], partial_s[:, m, 512:1024],
            ALU.mult, ALU.add)))(mt), drain=False)
        st_eb[("vstt", i)] = mark("vector")

    ew1_load(0)
    ew1_load(1)
    ew2_load(0)
    ew3_load(0)
    for i in range(NI + 1):
        ei, mt = divmod(i, MT)
        if i < NI:
            s1_h(i)
            if mt == MT - 1 and ei + 1 < E:
                ew2_load(ei + 1)
                if ei + 2 < E:
                    ew1_load(ei + 2)
        if i - 1 >= 0:
            s3_z(i - 1)
            ei1, mt1 = divmod(i - 1, MT)
            if mt1 == MT - 1 and ei1 + 1 < E:
                ew3_load(ei1 + 1)
        if i < NI:
            s2_vs(i)

    for mt in range(MT):
        op("tensor", (lambda m: (lambda e: e.matmul(
            pb[4][:], selt_s[0:E, m, :], eb3_s[:, 0:512],
            start=True, stop=True, skip_group_check=True)))(mt),
            waits=([("ve", v_selt), ("ve", st_eb[("vstt", NI - 1)])] if mt == 0
                   else [("ve", V("ve"))]))
        op("tensor", (lambda m: (lambda e: e.matmul(
            pb[5][:], selt_s[0:E, m, :], eb3_s[:, 512:1024],
            start=True, stop=True, skip_group_check=True)))(mt))
        pe_eb3 = mark("tensor")
        op("vector", (lambda m: (lambda e: e.scalar_tensor_tensor(
            partial_s[:, m, 0:512], pb[4][:], wex_s[:, m:m + 1], partial_s[:, m, 0:512],
            ALU.mult, ALU.add)))(mt), waits=[("pe", pe_eb3)], drain=False)
        op("vector", (lambda m: (lambda e: e.scalar_tensor_tensor(
            partial_s[:, m, 512:1024], pb[5][:], wex_s[:, m:m + 1], partial_s[:, m, 512:1024],
            ALU.mult, ALU.add)))(mt), drain=False)
        mark("vector")
    v_fin = mark("vector")
    dma("sync", domoe.rearrange("(m p) d -> p m d", p=128), partial_s[:],
        waits=[("ve", v_fin)], sem="oc")

    sch.emit("sync", lambda e: e.nop(),
             waits=[("oa", V("oa")), ("ob", V("ob")), ("oc", V("oc")), ("od", V("od"))])
    sch.emit("gpsimd", lambda e: e.nop(),
             waits=[("sw", V("sw")), ("wa", V("wa")), ("wb", V("wb")), ("wc", V("wc"))])

    with nc.Block() as block:
        def make_replay(name):
            def rep(eng):
                seen = {}
                for fn, waits, inc, amt, drain in sch.ops[name]:
                    for (s, v) in waits:
                        if seen.get(s, 0) < v:
                            eng.wait_ge(sems[s], v)
                            seen[s] = v
                    if (drain or RACE_DEBUG) and name in ("vector", "scalar", "gpsimd"):
                        eng.drain()
                    inst = fn(eng)
                    if inc is not None:
                        if inst is None:
                            inst = eng.nop()
                        inst.then_inc(sems[inc], amt)
            return rep

        for name in ENGINES:
            getattr(block, name)(make_replay(name))

    es.close()
    return nc


_NC_CACHE = None


def kernel(**inp):
    global _NC_CACHE
    f32 = np.float32
    import ml_dtypes
    bf = ml_dtypes.bfloat16
    C = np.ascontiguousarray

    x = np.asarray(inp["L0_attn_x"], dtype=f32)
    ib = np.asarray(inp["L0_ffn_index_b"]).astype(np.int32)
    it = np.asarray(inp["L0_ffn_index_t"]).astype(np.int32)

    shared = {
        "wq": C(inp["L0_attn_Wq"], dtype=f32),
        "wk": C(inp["L0_attn_Wk"], dtype=f32),
        "wv": C(inp["L0_attn_Wv"], dtype=f32),
        "wo": C(inp["L0_attn_Wo"], dtype=f32),
        "sw1t": C(np.asarray(inp["L0_ffn_sw1"]).T, dtype=f32),
        "sw2t": C(np.asarray(inp["L0_ffn_sw2"]).T, dtype=f32),
        "sw3t": C(np.asarray(inp["L0_ffn_sw3"]).T, dtype=f32),
        "ew1t": C(np.asarray(inp["L0_ffn_ew1"]).transpose(0, 2, 1), dtype=f32),
        "ew2t": C(np.asarray(inp["L0_ffn_ew2"]).transpose(0, 2, 1), dtype=f32),
        "ew3t": C(np.asarray(inp["L0_ffn_ew3"]).transpose(0, 2, 1), dtype=f32),
        "eb1": C(inp["L0_ffn_eb1"], dtype=f32),
        "eb2": C(inp["L0_ffn_eb2"], dtype=f32),
        "eb3": C(inp["L0_ffn_eb3"], dtype=f32),
        "gsw": C(np.concatenate([np.asarray(inp["L0_ffn_eg_w"], dtype=f32).T,
                                 np.asarray(inp["L0_ffn_sg_w"], dtype=f32).T], axis=1)),
        "gb9": C(np.broadcast_to(np.concatenate(
            [np.asarray(inp["L0_ffn_eg_b"], dtype=f32),
             np.asarray(inp["L0_ffn_sg_b"], dtype=f32)])[None, :], (128, E + 1))),
        "expb": C(np.broadcast_to(
            np.asarray(inp["L0_ffn_exp_bias"], dtype=f32)[None, :], (128, E))),
        "iota8": C(np.broadcast_to(np.arange(E, dtype=f32)[None, :], (128, E))),
        "iota16": C(np.broadcast_to(np.arange(16, dtype=f32)[None, :], (128, 16))),
        "selg": C(np.broadcast_to(
            np.asarray(inp["L0_ffn_sel_ln_g"], dtype=f32)[None, :], (128, H))),
        "selb": C(np.broadcast_to(
            np.asarray(inp["L0_ffn_sel_ln_b"], dtype=f32)[None, :], (128, H))),
        "slng": C(np.asarray(inp["L0_ffn_s_ln_g"], dtype=f32).reshape(4, 128).T),
        "slnb": C(np.asarray(inp["L0_ffn_s_ln_b"], dtype=f32).reshape(4, 128).T),
        "sb1": C(np.asarray(inp["L0_ffn_sb1"], dtype=f32)[None, :]),
        "sb2": C(np.asarray(inp["L0_ffn_sb2"], dtype=f32)[None, :]),
        "sb3": C(np.asarray(inp["L0_ffn_sb3"], dtype=f32)[None, :]),
        "bo": C(np.asarray(inp["L0_attn_bo"], dtype=f32)[None, :]),
        "idn": np.eye(128, dtype=bf),
        "idnf": np.eye(128, dtype=f32),
        "onesr": np.ones((1, 128), dtype=bf),
        "onesc": np.ones((128, 1), dtype=bf),
    }

    in_maps = []
    for c in range(NCORES):
        m = dict(shared)
        m["x"] = C(x[2 * c:2 * c + 2].reshape(BPC * N, D))
        sl = slice(c * MS, (c + 1) * MS)
        m["xmoe"] = C(x[ib[sl], it[sl]])
        m["idb"] = C(ib[sl].reshape(MT, 128).T)
        in_maps.append(m)

    if _NC_CACHE is None:
        _NC_CACHE = build_program()
    res = run_bass_kernel_spmd(_NC_CACHE, in_maps, core_ids=list(range(NCORES)))

    out = np.empty((B, N, D), dtype=f32)
    moe = np.empty((M, D), dtype=f32)
    for c in range(NCORES):
        out[2 * c:2 * c + 2] = np.asarray(res.results[c]["om"]).reshape(BPC, N, D)
        moe[c * MS:(c + 1) * MS] = np.asarray(res.results[c]["omoe"])
    out[ib, it] = moe
    return out
